# version 13
# speedup vs baseline: 1.1233x; 1.0091x over previous
import numpy as np
import ml_dtypes
from contextlib import ExitStack

import concourse.bass as bass
import concourse.mybir as mybir
from concourse.bass_utils import run_bass_kernel_spmd

F32 = mybir.dt.float32
BF16 = mybir.dt.bfloat16
FP16 = mybir.dt.float16
FP8 = mybir.dt.float8e4
AX = mybir.AxisListType.X
AF = mybir.ActivationFunctionType
OP = mybir.AluOpType

MAGIC = float(np.float32(1.5 * 2 ** 23))
S = 2048
DM = 1024
NCH = 16
NG = 4
KT = 8
I0 = 2816
NI0 = 22
H1 = 4096
NH1 = 32
NCORES = 8
RUN_LAYERS = (0, 1)
L1_PARTS = ("g", "l", "ffn")
SBUF_BASE = 16512
SBUF_LIMIT = 212832
ENGS = ("sync", "gpsimd", "vector", "scalar", "tensor")


class Emitter:
    def __init__(self, nc):
        self.nc = nc
        self.q = {e: [] for e in ENGS}
        self.sems = {}
        self.cnt = {}

    def add_sem(self, name, handle):
        self.sems[name] = handle
        self.cnt[name] = 0

    def now(self, sem):
        return self.cnt[sem]

    def wait(self, eng, sem, val):
        if val <= 0:
            return
        assert val <= self.cnt[sem], f"future wait {sem} {val}>{self.cnt[sem]}"
        s = self.sems[sem]
        self.q[eng].append(lambda e, s=s, v=val: e.wait_ge(s, v))

    def op(self, eng, fn, sem=None, amt=1):
        if sem is None:
            self.q[eng].append(fn)
            return None
        s = self.sems[sem]
        prev = self.cnt[sem]
        if prev > 0 and amt == 1:
            self.q[eng].append(lambda e, s=s, v=prev: e.wait_ge(s, v))
        self.cnt[sem] += amt
        self.q[eng].append(lambda e, f=fn, s=s, a=amt: f(e).then_inc(s, a))
        return self.cnt[sem]

    def v(self, fn):
        return self.op("vector", fn, "v")

    def sc(self, fn):
        return self.op("scalar", fn, "sc")

    def pe(self, fn, inc=True):
        return self.op("tensor", fn, "pe" if inc else None)

    def gp(self, fn):
        return self.op("gpsimd", fn, "gp")

    def dma(self, eng, out, in_, sem):
        prev = self.cnt[sem]
        if prev > 0:
            s = self.sems[sem]
            self.q[eng].append(lambda e, s=s, v=prev: e.wait_ge(s, v))
        return self.op(eng, lambda e, o=out, i=in_: e.dma_start(out=o, in_=i),
                       sem, 16)

    def play(self):
        with self.nc.Block() as blk:
            for eng in ENGS:
                thunks = self.q[eng]

                def body(e, th=thunks):
                    for t in th:
                        t(e)
                getattr(blk, eng)(body)


class Arena:
    def __init__(self, nc, base, limit=SBUF_LIMIT):
        self.nc = nc
        self.off = SBUF_BASE + base
        self.limit = SBUF_BASE + limit

    def take(self, name, shape, dtype):
        per = int(np.prod(shape[1:])) * mybir.dt.size(dtype)
        off = (self.off + 31) & ~31
        assert off + per <= self.limit, f"SBUF overflow {name}: {off}+{per}"
        self.off = off + per
        return self.nc.alloc_sbuf_tensor_at(name, list(shape), dtype, offset=off)


def ap3(t, offset, dims):
    return bass.AP(t, offset, dims)


def build_nc(layers=(0, 1), dbg=None):
    nc = bass.Bass("TRN2", target_bir_lowering=False, debug=False)
    d = {}

    def dram(name, shape, dtype, kind="ExternalInput"):
        d[name] = nc.dram_tensor(name, list(shape), dtype, kind=kind)
        return d[name]

    dram("x_in", [S, DM], F32)
    for w in ("wq_t", "wk_t", "wv_t", "wo_t"):
        dram(w, [DM, DM], F32)
    dram("bq", [128, 8], F32)
    dram("bk", [128, 8], F32)
    dram("bv_row", [1, DM], F32)
    dram("bo_row", [1, DM], F32)
    dram("clip_x", [1, 1], F32)
    dram("clip_down", [1, 1], F32)
    dram("wg", [I0, DM], F32)
    dram("wu", [I0, DM], F32)
    dram("wd", [DM, I0], F32)
    dram("n1w", [128, 8], F32)
    dram("n2w", [128, 8], F32)
    dram("gwq", [DM, DM], F32)
    dram("gkvd", [DM, 256], F32)
    dram("gkup", [256, 256], F32)
    dram("gvup", [256, 256], F32)
    dram("gwo", [DM, DM], F32)
    dram("lwq", [DM, DM], F32)
    dram("lwk", [DM, 256], F32)
    dram("lwv", [DM, 256], F32)
    dram("lwo", [DM, DM], F32)
    dram("w1_t", [DM, H1], F32)
    dram("b1", [128, 32], F32)
    dram("w2_t", [H1, DM], F32)
    dram("b2_row", [1, DM], F32)
    dram("ident_c", [128, 128], BF16)
    dram("causal4_c", [128, 512], BF16)
    dram("win_c", [128, 128], BF16)
    dram("ropes_g", [128, 1024], F32)
    dram("ropes_l", [128, 1024], F32)
    dram("drop_c", [S, DM], BF16)
    dram("out_x", [S, DM], F32, kind="ExternalOutput")
    if dbg:
        dram("dbg0", [S, DM], F32, kind="ExternalOutput")
    dram("xq_hbm", [DM, S], BF16, kind="Internal")
    dram("h_hbm", [I0, S], FP16, kind="Internal")
    dram("xq8_hbm", [DM, S], FP8, kind="Internal")
    dram("x_hbm", [S, DM], F32, kind="Internal")
    dram("cc1_in", [1, 128], F32, kind="Internal")
    dram("cc1_out", [1, 128], F32, kind="Internal")
    dram("cc2_in", [1, 128], F32, kind="Internal")
    dram("cc2_out", [1, 128], F32, kind="Internal")

    ctx = ExitStack()
    em = Emitter(nc)
    for sname in ("ld", "gw", "sd", "v", "sc", "pe", "gp", "cc"):
        em.add_sem(sname, ctx.enter_context(nc.semaphore(sname)))

    P = [ctx.enter_context(nc.psum_tensor(f"P{i}", [128, 512], F32))
         for i in range(6)]
    T6 = ctx.enter_context(nc.psum_tensor("T6", [128, 1024], BF16))
    T7 = ctx.enter_context(nc.psum_tensor("T7", [128, 1024], BF16))

    A = Arena(nc, 0)
    x_sb = A.take("x_sb", [128, NCH * DM], F32)
    sA = A.take("sA", [128, DM], F32)
    sB = A.take("sB", [128, DM], F32)
    tbf = A.take("tbf", [128, DM], BF16)
    tbf2 = A.take("tbf2", [128, DM], BF16)
    ystg = A.take("ystg", [64, DM], BF16)
    t8 = A.take("t8", [128, DM], FP8)
    m0 = A.take("m0", [128, 16], F32)
    f0 = A.take("f0", [128, 16], F32)
    nb0 = A.take("nb0", [128, 16], F32)
    sm_t = A.take("sm_t", [128, 16], F32)
    ss_t = A.take("ss_t", [128, 16], F32)
    m1 = A.take("m1", [128, 16], F32)
    f1 = A.take("f1", [128, 16], F32)
    nb1 = A.take("nb1", [128, 16], F32)
    fr1 = A.take("fr1", [128, 16], F32)
    absacc = A.take("absacc", [128, 1], F32)
    absacc2 = A.take("absacc2", [128, 1], F32)
    mxc = A.take("mxc", [128, 1], F32)
    hdf = A.take("hdf", [128, 16], F32)
    recd = A.take("recd", [1, 512], F32)
    mrow = A.take("mrow", [1, 128], F32)
    smt = A.take("smt", [1, 16], F32)
    qsc = A.take("qsc", [128, 1], F32)
    qbi = A.take("qbi", [128, 1], F32)
    rsx_b = A.take("rsx_b", [128, 1], F32)
    sx_b = A.take("sx_b", [128, 1], F32)
    rh_b = A.take("rh_b", [128, 1], F32)
    sh_b = A.take("sh_b", [128, 1], F32)
    ident = A.take("ident", [128, 128], BF16)
    causal4 = A.take("causal4", [128, 512], BF16)
    winm = A.take("winm", [128, 128], BF16)
    onescol = A.take("onescol", [128, 1], BF16)
    ones_row_bf = A.take("ones_row_bf", [1, 128], BF16)
    ones_row_f = A.take("ones_row_f", [1, 128], F32)
    zrow_bf = A.take("zrow_bf", [1, 512], BF16)
    bq_sb = A.take("bq_sb", [128, 8], F32)
    bk_sb = A.take("bk_sb", [128, 8], F32)
    n1w_sb = A.take("n1w_sb", [128, 8], F32)
    n2w_sb = A.take("n2w_sb", [128, 8], F32)
    clipx_sb = A.take("clipx_sb", [1, 1], F32)
    clipd_sb = A.take("clipd_sb", [1, 1], F32)
    rowstg = A.take("rowstg", [1, DM], F32)
    bvrow = A.take("bvrow", [1, DM], BF16)
    borow = A.take("borow", [1, DM], BF16)
    b1q8 = A.take("b1q8", [128, 32], FP8)
    b1q = A.take("b1q", [128, 32], F32)
    magic_b = A.take("magic_b", [128, 1], F32)
    neg8_b = A.take("neg8_b", [128, 1], F32)
    recb = A.take("recb", [64, 512], F32)
    PHASE_BASE = (A.off + 63) & ~63

    em.dma("sync", ident[:, :], d["ident_c"].ap(), "ld")
    em.dma("sync", causal4[:, :], d["causal4_c"].ap(), "ld")
    em.dma("sync", winm[:, :], d["win_c"].ap(), "ld")
    em.dma("sync", bq_sb[:, :], d["bq"].ap(), "ld")
    em.dma("sync", bk_sb[:, :], d["bk"].ap(), "ld")
    em.dma("sync", n1w_sb[:, :], d["n1w"].ap(), "ld")
    em.dma("sync", n2w_sb[:, :], d["n2w"].ap(), "ld")
    em.dma("sync", clipx_sb[:, :], d["clip_x"].ap(), "ld")
    em.dma("sync", clipd_sb[:, :], d["clip_down"].ap(), "ld")
    bv_ld = em.dma("sync", rowstg[:, :], d["bv_row"].ap(), "ld")
    em.dma("sync", b1q[:, :], d["b1"].ap(), "ld")
    ld_consts = em.now("ld")
    x_ld = []
    for c in range(NCH):
        x_ld.append(em.dma("sync", x_sb[:, c * DM:(c + 1) * DM],
                           d["x_in"].ap()[c * 128:(c + 1) * 128, :], "ld"))
    em.wait("vector", "ld", ld_consts)
    em.v(lambda e: e.memset(onescol[:, :], 1.0))
    em.v(lambda e: e.memset(ones_row_bf[:, :], 1.0))
    em.v(lambda e: e.memset(ones_row_f[:, :], 1.0))
    em.v(lambda e: e.memset(zrow_bf[:, :], 0.0))
    em.v(lambda e: e.tensor_copy(bvrow[:, :], rowstg[:, :]))
    em.wait("sync", "v", em.now("v"))
    bo_ld = em.dma("sync", rowstg[:, :], d["bo_row"].ap(), "ld")
    em.wait("vector", "ld", bo_ld)
    em.v(lambda e: e.tensor_copy(borow[:, :], rowstg[:, :]))
    em.v(lambda e: e.memset(absacc[:, :], 0.0))
    em.v(lambda e: e.memset(absacc2[:, :], 0.0))
    em.v(lambda e: e.memset(magic_b[:, :], MAGIC))
    em.v(lambda e: e.memset(neg8_b[:, :], -8.0))
    init_v = em.now("v")

    def transpose8(src_bf, psum, n=8):
        for k in range(n):
            em.pe(lambda e, k=k: e.transpose(
                psum[:, k * 128:(k + 1) * 128],
                src_bf[:, k * 128:(k + 1) * 128], ident[:, :]))

    def ln_stats(c, mm_, ff, nnb, absmax=None, eps=1e-5):
        xc = x_sb[:, c * DM:(c + 1) * DM]
        em.wait("scalar", "ld", x_ld[c])
        em.sc(lambda e, xc=xc, c=c: e.activation(
            sA[:, :], xc, AF.Square, accum_out=ss_t[:, c:c + 1]))
        em.wait("vector", "ld", x_ld[c])
        em.v(lambda e, xc=xc, c=c: e.tensor_reduce(
            sm_t[:, c:c + 1], xc, AX, OP.add))
        sc_sq = em.now("sc")
        em.wait("vector", "sc", sc_sq)
        em.v(lambda e, c=c: e.tensor_scalar(
            mm_[:, c:c + 1], sm_t[:, c:c + 1], 1.0 / DM, None, OP.mult))
        em.v(lambda e, c=c: e.tensor_scalar(
            sB[:, 0:1], ss_t[:, c:c + 1], 1.0 / DM, eps, OP.mult, OP.add))
        em.v(lambda e, c=c: e.tensor_tensor(
            sB[:, 1:2], mm_[:, c:c + 1], mm_[:, c:c + 1], OP.mult))
        em.v(lambda e: e.tensor_sub(sB[:, 0:1], sB[:, 0:1], sB[:, 1:2]))
        em.v(lambda e: e.reciprocal(sB[:, 2:3], sB[:, 0:1]))
        vv = em.now("v")
        em.wait("scalar", "v", vv)
        em.sc(lambda e, c=c: e.activation(ff[:, c:c + 1], sB[:, 2:3], AF.Sqrt))
        sc_f = em.now("sc")
        em.wait("vector", "sc", sc_f)
        em.v(lambda e, c=c: e.tensor_tensor(
            sB[:, 3:4], mm_[:, c:c + 1], ff[:, c:c + 1], OP.mult))
        em.v(lambda e, c=c: e.tensor_scalar(
            nnb[:, c:c + 1], sB[:, 3:4], -1.0, None, OP.mult))
        if absmax is not None:
            em.v(lambda e, xc=xc, c=c: e.tensor_scalar(
                sB[:, :], xc, mm_[:, c:c + 1], ff[:, c:c + 1],
                OP.subtract, OP.mult))
            em.v(lambda e: e.tensor_reduce(mxc[:, :], sB[:, :], AX, OP.max,
                                           apply_absolute_value=True))
            em.v(lambda e, am=absmax: e.tensor_tensor(
                am[:, :], am[:, :], mxc[:, :], OP.max))

    def allreduce_max(acc_sb, cin, cout, col):
        vv = em.now("v")
        em.wait("gpsimd", "v", vv)
        em.dma("gpsimd", cin.ap()[0, :], acc_sb[:, 0], "gw")
        em.wait("gpsimd", "gw", em.now("gw"))
        em.op("gpsimd", lambda e: e.collective_compute(
            "AllReduce", OP.max, replica_groups=[list(range(NCORES))],
            ins=[cin.ap().opt()], outs=[cout.ap().opt()]), "cc", 1)
        em.wait("gpsimd", "cc", em.now("cc"))
        em.dma("gpsimd", mrow[:, :], cout.ap(), "gw")
        em.wait("vector", "gw", em.now("gw"))
        em.v(lambda e, col=col: e.tensor_reduce(
            smt[:, col:col + 1], mrow[:, :], AX, OP.max))

    def bcast_scalar(src_col, dst_sb):
        vv = em.now("v")
        em.wait("tensor", "v", vv)
        em.pe(lambda e: e.matmul(P[5][:, 0:1], ones_row_f[:, :],
                                 smt[0:1, src_col:src_col + 1],
                                 start=True, stop=True))
        pen = em.now("pe")
        em.wait("vector", "pe", pen)
        em.v(lambda e, dst_sb=dst_sb: e.tensor_copy(dst_sb[:, :], P[5][:, 0:1]))

    em.wait("vector", "v", init_v)
    for c in range(NCH):
        ln_stats(c, m0, f0, nb0, absmax=absacc)

    allreduce_max(absacc, d["cc1_in"], d["cc1_out"], 0)
    em.wait("scalar", "ld", ld_consts)
    em.wait("scalar", "v", em.now("v"))
    em.sc(lambda e: e.activation(smt[:, 4:5], clipx_sb[:, :], AF.Sigmoid))
    em.wait("vector", "sc", em.now("sc"))
    em.v(lambda e: e.tensor_tensor(smt[:, 5:6], smt[:, 4:5], smt[:, 0:1],
                                   OP.mult))
    em.v(lambda e: e.tensor_scalar(smt[:, 6:7], smt[:, 5:6], 1.0 / 127.0,
                                   None, OP.mult))
    em.v(lambda e: e.reciprocal(smt[:, 7:8], smt[:, 6:7]))
    bcast_scalar(7, rsx_b)
    bcast_scalar(6, sx_b)
    vscale1 = em.now("v")

    em.wait("vector", "v", vscale1)
    for c in range(NCH):
        xc = x_sb[:, c * DM:(c + 1) * DM]
        em.v(lambda e, c=c: e.tensor_tensor(qsc[:, :], f0[:, c:c + 1],
                                            rsx_b[:, :], OP.mult))
        em.v(lambda e, c=c: e.tensor_tensor(sB[:, 0:1], m0[:, c:c + 1],
                                            qsc[:, :], OP.mult))
        em.v(lambda e: e.tensor_scalar(qbi[:, :], sB[:, 0:1], -1.0, MAGIC,
                                       OP.mult, OP.add))
        vv = em.now("v")
        em.wait("scalar", "v", vv)
        em.sc(lambda e, xc=xc: e.activation(sA[:, :], xc, AF.Identity,
                                            bias=qbi[:, :], scale=qsc[:, :]))
        em.wait("vector", "sc", em.now("sc"))
        em.v(lambda e: e.tensor_scalar(sB[:, :], sA[:, :], -MAGIC, 127.0,
                                       OP.add, OP.min))
        em.v(lambda e: e.tensor_scalar(tbf[:, :], sB[:, :], -128.0, None,
                                       OP.max))
        em.wait("tensor", "v", em.now("v"))
        em.wait("tensor", "sd", max(0, em.now("sd")))
        transpose8(tbf, T6)
        em.wait("vector", "pe", em.now("pe"))
        em.v(lambda e: e.tensor_copy(tbf2[:, :], T6[:, :]))
        em.wait("scalar", "v", em.now("v"))
        dst = ap3(d["xq_hbm"], c * 128, [[S, 128], [128 * S, 8], [1, 128]])
        em.dma("scalar", dst, ap3(tbf2, 0, [[DM, 128], [128, 8], [1, 128]]),
               "sd")
    b1_v = em.now("v")
    b1_sd = em.now("sd")

    AR = Arena(nc, PHASE_BASE - SBUF_BASE)
    wq_bf = AR.take("wq_bf", [128, KT * DM], BF16)
    wk_bf = AR.take("wk_bf", [128, KT * DM], BF16)
    wv_bf = AR.take("wv_bf", [128, KT * DM], BF16)
    wo_bf = AR.take("wo_bf", [128, KT * DM], BF16)
    xnT_g = AR.take("xnT_g", [128, KT * 512], BF16)
    qT_g = AR.take("qT_g", [128, KT * 512], BF16)
    kT_g = AR.take("kT_g", [128, KT * 512], BF16)
    v_g = AR.take("v_g", [128, 4 * 1040], BF16)
    kA_c = AR.take("kA_c", [128, DM], BF16)
    STb = [AR.take(f"STb{i}", [128, 512], BF16) for i in range(2)]
    state_sb = AR.take("state_sb", [128, 8 * 65], BF16)
    state32 = AR.take("state32", [128, 8 * 65], F32)
    yT_c = AR.take("yT_c", [128, DM], BF16)

    for wname, wsb in (("wq_t", wq_bf), ("wk_t", wk_bf), ("wv_t", wv_bf),
                       ("wo_t", wo_bf)):
        src = ap3(d[wname], 0, [[DM, 128], [128 * DM, KT], [1, DM]])
        dst = ap3(wsb, 0, [[KT * DM, 128], [DM, KT], [1, DM]])
        em.dma("gpsimd", dst, src, "gw")
    l0w_ld = em.now("gw")
    em.v(lambda e: e.memset(
        ap3(v_g, 64, [[4 * 1040, 128], [1040, 4], [65, 16]]), 1.0))
    vg_ones = em.now("v")

    for g in range(NG):
        for cl in range(4):
            c = 4 * g + cl
            xc = x_sb[:, c * DM:(c + 1) * DM]
            em.wait("scalar", "v", b1_v)
            em.wait("scalar", "pe", em.now("pe"))
            em.sc(lambda e, xc=xc, c=c: e.activation(
                tbf[:, :], xc, AF.Identity, bias=nb0[:, c:c + 1],
                scale=f0[:, c:c + 1]))
            em.wait("tensor", "sc", em.now("sc"))
            em.wait("tensor", "v", em.now("v"))
            transpose8(tbf, T6)
            em.wait("vector", "pe", em.now("pe"))
            em.v(lambda e, cl=cl: e.tensor_copy(
                ap3(xnT_g, 128 * cl, [[KT * 512, 128], [512, 8], [1, 128]]),
                ap3(T6, 0, [[1024, 128], [128, 8], [1, 128]])))
        em.wait("tensor", "v", em.now("v"))
        em.wait("tensor", "gw", l0w_ld)

        b2_cons_sc = {0: em.now("sc"), 1: em.now("sc")}
        for wsb, dstT, bias in ((wq_bf, qT_g, bq_sb), (wk_bf, kT_g, bk_sb)):
            for ot in range(8):
                pb = P[ot % 2]
                em.wait("tensor", "sc", b2_cons_sc[ot % 2])
                for kt in range(KT):
                    em.pe(lambda e, pb=pb, wsb=wsb, ot=ot, kt=kt: e.matmul(
                        pb[:, :],
                        wsb[:, DM * kt + 128 * ot: DM * kt + 128 * (ot + 1)],
                        xnT_g[:, 512 * kt:512 * (kt + 1)],
                        start=(kt == 0), stop=(kt == KT - 1)))
                pen = em.now("pe")
                em.wait("scalar", "pe", pen)
                em.sc(lambda e, pb=pb, dstT=dstT, bias=bias, ot=ot:
                      e.activation(dstT[:, 512 * ot:512 * (ot + 1)], pb[:, :],
                                   AF.Exp, bias=bias[:, ot:ot + 1]))
                b2_cons_sc[ot % 2] = em.now("sc")
        em.wait("tensor", "v", vg_ones)
        for cl in range(4):
            for oh in range(2):
                pb = P[2 + (2 * cl + oh) % 2]
                em.wait("tensor", "v", em.now("v"))
                for kt in range(KT):
                    em.pe(lambda e, pb=pb, cl=cl, oh=oh, kt=kt: e.matmul(
                        pb[:, :],
                        xnT_g[:, 512 * kt + 128 * cl: 512 * kt + 128 * (cl + 1)],
                        wv_bf[:, DM * kt + 512 * oh: DM * kt + 512 * (oh + 1)],
                        start=(kt == 0), stop=False))
                em.pe(lambda e, pb=pb, oh=oh: e.matmul(
                    pb[:, :], ones_row_bf[:, :],
                    bvrow[:, 512 * oh:512 * (oh + 1)], start=False, stop=True))
                pen = em.now("pe")
                em.wait("vector", "pe", pen)
                em.v(lambda e, pb=pb, cl=cl, oh=oh: e.tensor_copy(
                    ap3(v_g, 1040 * cl + 65 * 8 * oh,
                        [[4 * 1040, 128], [65, 8], [1, 64]]),
                    ap3(pb, 0, [[512, 128], [64, 8], [1, 64]])))
        qkv_v = em.now("v")
        qkv_sc = em.now("sc")

        for cl in range(4):
            c = 4 * g + cl
            em.wait("tensor", "sc", qkv_sc)
            em.wait("tensor", "v", em.now("v"))
            for ot in range(8):
                em.pe(lambda e, ot=ot, cl=cl: e.transpose(
                    T7[:, 128 * ot:128 * (ot + 1)],
                    kT_g[:, 512 * ot + 128 * cl: 512 * ot + 128 * (cl + 1)],
                    ident[:, :]))
            em.wait("vector", "pe", em.now("pe"))
            em.wait("vector", "v", qkv_v)
            em.v(lambda e: e.tensor_copy(kA_c[:, :], T7[:, :]))
            ka_v = em.now("v")
            em.wait("tensor", "v", ka_v)

            for hb in range(4):
                ot_bank = P[2 + hb % 2]
                em.wait("tensor", "v", em.now("v"))
                for hh in range(4):
                    h = 4 * hb + hh
                    po, pof = h // 2, 64 * (h % 2)
                    em.pe(lambda e, hh=hh, po=po, pof=pof, cl=cl: e.matmul(
                        P[0][:, 128 * hh:128 * (hh + 1)],
                        kT_g[pof:pof + 64,
                             512 * po + 128 * cl: 512 * po + 128 * (cl + 1)],
                        qT_g[pof:pof + 64,
                             512 * po + 128 * cl: 512 * po + 128 * (cl + 1)],
                        start=True, stop=True))
                em.wait("vector", "pe", em.now("pe"))
                em.v(lambda e, hb=hb: e.tensor_tensor(
                    STb[hb % 2][:, :], P[0][:, :], causal4[:, :], OP.mult))
                em.wait("tensor", "v", em.now("v"))
                for hh in range(4):
                    h = 4 * hb + hh
                    po, pof = h // 2, 64 * (h % 2)
                    em.pe(lambda e, h=h, hh=hh, hb=hb, cl=cl, c=c,
                          ot_bank=ot_bank: e.matmul(
                        ot_bank[0:65, 128 * hh:128 * (hh + 1)],
                        v_g[:, 1040 * cl + 65 * h: 1040 * cl + 65 * h + 65],
                        STb[hb % 2][:, 128 * hh:128 * (hh + 1)],
                        start=True, stop=(c == 0)))
                    if c > 0:
                        em.pe(lambda e, hh=hh, cl=cl, po=po, pof=pof,
                              ot_bank=ot_bank: e.matmul(
                            ot_bank[0:65, 128 * hh:128 * (hh + 1)],
                            state_sb[pof:pof + 64, 65 * po:65 * po + 65],
                            qT_g[pof:pof + 64,
                                 512 * po + 128 * cl: 512 * po + 128 * (cl + 1)],
                            start=False, stop=True))
                if c < NCH - 1:
                    for hh in range(4):
                        h = 4 * hb + hh
                        po, pof = h // 2, 64 * (h % 2)
                        em.pe(lambda e, h=h, cl=cl, po=po, pof=pof: e.matmul(
                            P[4][pof:pof + 64, 64 * po:64 * (po + 1)],
                            kA_c[:, 64 * h:64 * (h + 1)],
                            v_g[:, 1040 * cl + 65 * h: 1040 * cl + 65 * h + 64],
                            start=True, stop=True))
                        em.pe(lambda e, h=h, po=po, pof=pof, cl=cl: e.matmul(
                            P[5][pof:pof + 64, po:po + 1],
                            kA_c[:, 64 * h:64 * (h + 1)], onescol[:, :],
                            start=True, stop=True))
                pen = em.now("pe")
                em.wait("vector", "pe", pen)
                em.v(lambda e, ot_bank=ot_bank: e.reciprocal(
                    recd[:, :], ot_bank[64:65, :]))
                em.wait("tensor", "v", em.now("v"))
                em.pe(lambda e: e.matmul(P[1][0:64, :], ones_row_f[0:1, 0:64],
                                         recd[:, :], start=True, stop=True))
                em.wait("vector", "pe", em.now("pe"))
                em.v(lambda e: e.tensor_copy(recb[0:64, :], P[1][0:64, :]))
                for hh in range(4):
                    h = 4 * hb + hh
                    if h % 2 == 0:
                        dstap = yT_c[0:64, 128 * (h // 2):128 * (h // 2) + 128]
                    else:
                        dstap = ystg[0:64, 128 * (h // 2):128 * (h // 2) + 128]
                    em.v(lambda e, hh=hh, dstap=dstap, ot_bank=ot_bank:
                         e.tensor_tensor(
                        dstap, ot_bank[0:64, 128 * hh:128 * (hh + 1)],
                        recb[0:64, 128 * hh:128 * (hh + 1)], OP.mult))
            if c < NCH - 1:
                em.wait("vector", "pe", em.now("pe"))
                if c == 0:
                    em.v(lambda e: e.tensor_copy(
                        ap3(state32, 0, [[520, 128], [65, 8], [1, 64]]),
                        ap3(P[4], 0, [[512, 128], [64, 8], [1, 64]])))
                    em.v(lambda e: e.tensor_copy(
                        ap3(state32, 64, [[520, 128], [65, 8]]),
                        ap3(P[5], 0, [[512, 128], [1, 8]])))
                else:
                    em.v(lambda e: e.tensor_tensor(
                        ap3(state32, 0, [[520, 128], [65, 8], [1, 64]]),
                        ap3(state32, 0, [[520, 128], [65, 8], [1, 64]]),
                        ap3(P[4], 0, [[512, 128], [64, 8], [1, 64]]),
                        OP.add))
                    em.v(lambda e: e.tensor_tensor(
                        ap3(state32, 64, [[520, 128], [65, 8]]),
                        ap3(state32, 64, [[520, 128], [65, 8]]),
                        ap3(P[5], 0, [[512, 128], [1, 8]]),
                        OP.add))
                em.v(lambda e: e.tensor_copy(state_sb[:, :], state32[:, :]))
            em.wait("sync", "v", em.now("v"))
            em.dma("sync", yT_c[64:128, :], ystg[0:64, :], "ld")
            em.wait("tensor", "ld", em.now("ld"))
            for oh in range(2):
                pb = P[0] if oh == 0 else P[1]
                for kt in range(KT):
                    em.pe(lambda e, pb=pb, kt=kt, oh=oh: e.matmul(
                        pb[:, :], yT_c[:, 128 * kt:128 * (kt + 1)],
                        wo_bf[:, DM * kt + 512 * oh: DM * kt + 512 * (oh + 1)],
                        start=(kt == 0), stop=False))
                em.pe(lambda e, pb=pb, oh=oh: e.matmul(
                    pb[:, :], ones_row_bf[:, :],
                    borow[:, 512 * oh:512 * (oh + 1)], start=False, stop=True))
                pen = em.now("pe")
                em.wait("vector", "pe", pen)
                xslice = x_sb[:, c * DM + 512 * oh: c * DM + 512 * (oh + 1)]
                em.v(lambda e, xs=xslice, pb=pb: e.tensor_add(xs, xs, pb[:, :]))
    attn_done_v = em.now("v")
    attn_done_pe = em.now("pe")

    AR = Arena(nc, PHASE_BASE - SBUF_BASE)
    wgT_q = AR.take("wgT_q", [128, KT * I0], BF16)
    wuT_q = AR.take("wuT_q", [128, KT * I0], BF16)
    wstg = [AR.take(f"wstg{i}", [128, DM], F32) for i in range(2)]
    xqtb = AR.take("xqtb", [128, KT * 512], BF16)
    h16t = [AR.take(f"h16t{i}", [128, 512], FP16) for i in range(2)]

    em.wait("sync", "pe", attn_done_pe)
    em.wait("sync", "v", attn_done_v)
    stg_consumed = [0, 0]
    for wi, (wname, wdst) in enumerate((("wg", wgT_q), ("wu", wuT_q))):
        for it in range(NI0):
            j = wi * NI0 + it
            stg = wstg[j % 2]
            em.wait("sync", "sc", stg_consumed[j % 2])
            ldt = em.dma("sync", stg[:, :],
                         d[wname].ap()[128 * it:128 * (it + 1), :], "ld")
            em.wait("vector", "ld", ldt)
            em.v(lambda e, stg=stg: e.tensor_reduce(
                mxc[:, :], stg[:, :], AX, OP.max, apply_absolute_value=True))
            em.v(lambda e: e.reciprocal(qsc[:, :], mxc[:, :]))
            em.v(lambda e: e.tensor_scalar(qsc[:, :], qsc[:, :], 127.0, None,
                                           OP.mult))
            em.v(lambda e: e.tensor_scalar(qbi[:, :], mxc[:, :], 1.0 / 127.0,
                                           None, OP.mult))
            vv = em.now("v")
            em.wait("scalar", "v", vv)
            em.sc(lambda e, stg=stg: e.activation(
                sA[:, :], stg[:, :], AF.Identity, bias=magic_b[:, :],
                scale=qsc[:, :]))
            stg_consumed[j % 2] = em.now("sc")
            em.wait("vector", "sc", em.now("sc"))
            em.v(lambda e: e.tensor_scalar(tbf[:, :], sA[:, :], -MAGIC,
                                           qbi[:, :], OP.add, OP.mult))
            em.wait("tensor", "v", em.now("v"))
            transpose8(tbf, T6)
            em.wait("vector", "pe", em.now("pe"))
            em.v(lambda e, wdst=wdst, it=it: e.tensor_copy(
                ap3(wdst, 128 * it, [[KT * I0, 128], [I0, 8], [1, 128]]),
                ap3(T6, 0, [[1024, 128], [128, 8], [1, 128]])))
    wgq_v = em.now("v")

    em.wait("sync", "sd", b1_sd)
    gu_cons_sc = {0: em.now("sc"), 1: em.now("sc")}
    gu_cons_v = {0: em.now("v"), 1: em.now("v")}
    for g in range(NG):
        src = ap3(d["xq_hbm"], 512 * g, [[S, 128], [128 * S, KT], [1, 512]])
        dstx = ap3(xqtb, 0, [[KT * 512, 128], [512, KT], [1, 512]])
        em.wait("sync", "pe", em.now("pe"))
        xq_ld = em.dma("sync", dstx, src, "ld")
        em.wait("tensor", "ld", xq_ld)
        em.wait("tensor", "v", wgq_v)
        for it in range(NI0):
            pg = P[it % 2]
            pu = P[2 + it % 2]
            em.wait("tensor", "sc", gu_cons_sc[it % 2])
            em.wait("tensor", "v", gu_cons_v[it % 2])
            for kt in range(KT):
                em.pe(lambda e, pg=pg, it=it, kt=kt: e.matmul(
                    pg[:, :],
                    wgT_q[:, I0 * kt + 128 * it: I0 * kt + 128 * (it + 1)],
                    xqtb[:, 512 * kt:512 * (kt + 1)],
                    start=(kt == 0), stop=(kt == KT - 1)))
            for kt in range(KT):
                em.pe(lambda e, pu=pu, it=it, kt=kt: e.matmul(
                    pu[:, :],
                    wuT_q[:, I0 * kt + 128 * it: I0 * kt + 128 * (it + 1)],
                    xqtb[:, 512 * kt:512 * (kt + 1)],
                    start=(kt == 0), stop=(kt == KT - 1)))
            pen = em.now("pe")
            em.wait("scalar", "pe", pen)
            em.sc(lambda e, pg=pg: e.activation(
                sA[:, 0:512], pg[:, :], AF.Silu, scale=sx_b[:, :]))
            gu_cons_sc[it % 2] = em.now("sc")
            em.wait("vector", "sc", em.now("sc"))
            hb_ = h16t[it % 2]
            em.wait("vector", "sd", max(0, em.now("sd") - 16))
            em.v(lambda e, pu=pu, hb_=hb_: e.tensor_tensor(
                hb_[:, :], sA[:, 0:512], pu[:, :], OP.mult))
            gu_cons_v[it % 2] = em.now("v")
            em.v(lambda e, hb_=hb_: e.tensor_reduce(
                mxc[:, :], hb_[:, :], AX, OP.max, apply_absolute_value=True))
            em.v(lambda e: e.tensor_tensor(absacc2[:, :], absacc2[:, :],
                                           mxc[:, :], OP.max))
            em.wait("scalar", "v", em.now("v"))
            em.dma("scalar",
                   d["h_hbm"].ap()[128 * it:128 * (it + 1),
                                   512 * g:512 * (g + 1)],
                   hb_[:, :], "sd")
    gu_pe = em.now("pe")
    gu_sd = em.now("sd")

    allreduce_max(absacc2, d["cc2_in"], d["cc2_out"], 1)
    em.wait("scalar", "v", em.now("v"))
    em.sc(lambda e: e.activation(smt[:, 8:9], clipd_sb[:, :], AF.Sigmoid))
    em.wait("vector", "sc", em.now("sc"))
    em.v(lambda e: e.tensor_tensor(smt[:, 9:10], smt[:, 8:9], smt[:, 1:2],
                                   OP.mult))
    em.v(lambda e: e.reciprocal(smt[:, 10:11], smt[:, 9:10]))
    em.v(lambda e: e.tensor_scalar(smt[:, 11:12], smt[:, 10:11], 127.0, None,
                                   OP.mult))
    em.v(lambda e: e.tensor_tensor(smt[:, 12:13], smt[:, 9:10], smt[:, 6:7],
                                   OP.mult))
    em.v(lambda e: e.tensor_scalar(smt[:, 13:14], smt[:, 12:13], 1.0 / 127.0,
                                   None, OP.mult))
    bcast_scalar(11, rh_b)
    bcast_scalar(13, sh_b)
    ffn2_scales_v = em.now("v")

    AR = Arena(nc, PHASE_BASE - SBUF_BASE)
    wdT_q = AR.take("wdT_q", [128, NI0 * DM], BF16)
    hq_bf = AR.take("hq_bf", [128, I0], BF16)
    hq2 = AR.take("hq2", [128, I0], BF16)
    hq_pair = [hq_bf, hq2]
    tD = AR.take("tD", [128, I0], F32)
    ov_base = AR.off - SBUF_BASE
    AR2 = Arena(nc, ov_base)
    wdstg = [AR2.take(f"wdstg{i}", [128, I0], F32) for i in range(2)]
    AR3 = Arena(nc, ov_base)
    hbuf = [AR3.take(f"hbuf{i}", [128, NI0 * 512], FP16) for i in range(2)]

    em.wait("sync", "pe", gu_pe)
    em.wait("sync", "v", em.now("v"))
    wd_consumed = [0, 0]
    for dt in range(8):
        stg = wdstg[dt % 2]
        em.wait("sync", "sc", wd_consumed[dt % 2])
        wd_ldt = em.dma("sync", stg[:, :],
                        d["wd"].ap()[128 * dt:128 * (dt + 1), :], "ld")
        em.wait("vector", "ld", wd_ldt)
        em.v(lambda e, stg=stg: e.tensor_reduce(
            mxc[:, :], stg[:, :], AX, OP.max, apply_absolute_value=True))
        em.v(lambda e: e.reciprocal(qsc[:, :], mxc[:, :]))
        em.v(lambda e: e.tensor_scalar(qsc[:, :], qsc[:, :], 127.0, None,
                                       OP.mult))
        em.v(lambda e: e.tensor_scalar(qbi[:, :], mxc[:, :], 1.0 / 127.0,
                                       None, OP.mult))
        em.wait("scalar", "v", em.now("v"))
        em.sc(lambda e, stg=stg: e.activation(
            tD[:, :], stg[:, :], AF.Identity, bias=magic_b[:, :],
            scale=qsc[:, :]))
        wd_consumed[dt % 2] = em.now("sc")
        em.wait("vector", "sc", em.now("sc"))
        em.v(lambda e: e.tensor_scalar(hq_bf[:, :], tD[:, :], -MAGIC,
                                       qbi[:, :], OP.add, OP.mult))
        for b0 in range(0, NI0, 8):
            nn = min(8, NI0 - b0)
            em.wait("tensor", "v", em.now("v"))
            for k in range(nn):
                em.pe(lambda e, b0=b0, k=k: e.transpose(
                    T6[:, 128 * k:128 * (k + 1)],
                    hq_bf[:, 128 * (b0 + k):128 * (b0 + k + 1)], ident[:, :]))
            em.wait("vector", "pe", em.now("pe"))
            em.v(lambda e, b0=b0, nn=nn, dt=dt: e.tensor_copy(
                ap3(wdT_q, DM * b0 + 128 * dt,
                    [[NI0 * DM, 128], [DM, nn], [1, 128]]),
                ap3(T6, 0, [[1024, 128], [128, nn], [1, 128]])))
    wd_v = em.now("v")

    em.wait("sync", "sd", gu_sd)
    em.wait("scalar", "v", wd_v)
    em.wait("sync", "sc", em.now("sc"))
    em.wait("sync", "v", em.now("v"))
    h_consumed = [0, 0]
    h_ldt_g = [0]
    hq_ready = {0: 0, 1: 0}

    def dp_quant(c):
        grp, cl_ = c // 4, c % 4
        hb_ = hbuf[grp % 2]
        if cl_ == 0:
            src = ap3(d["h_hbm"], 512 * grp,
                      [[S, 128], [128 * S, NI0], [1, 512]])
            dsth = ap3(hb_, 0, [[NI0 * 512, 128], [512, NI0], [1, 512]])
            em.wait("sync", "sc", h_consumed[grp % 2])
            h_ldt_g[0] = em.dma("sync", dsth, src, "ld")
        em.wait("scalar", "ld", h_ldt_g[0])
        em.wait("scalar", "v", ffn2_scales_v)
        em.wait("scalar", "v", em.now("v"))
        em.sc(lambda e, hb_=hb_, cl_=cl_: e.activation(
            ap3(tD, 0, [[I0, 128], [128, NI0], [1, 128]]),
            ap3(hb_, 128 * cl_, [[NI0 * 512, 128], [512, NI0], [1, 128]]),
            AF.Identity, bias=magic_b[:, :], scale=rh_b[:, :]))
        if cl_ == 3:
            h_consumed[grp % 2] = em.now("sc")
        em.wait("vector", "sc", em.now("sc"))
        em.v(lambda e: e.tensor_scalar(tD[:, :], tD[:, :], -MAGIC, 127.0,
                                       OP.add, OP.min))
        em.v(lambda e, hq=hq_pair[c % 2]: e.tensor_scalar(
            hq[:, :], tD[:, :], -128.0, None, OP.max))
        hq_ready[c % 2] = em.now("v")

    dp_quant(0)
    for c in range(NCH):
        if c + 1 < NCH:
            dp_quant(c + 1)
        em.wait("tensor", "v", hq_ready[c % 2])
        em.wait("tensor", "v", wd_v)
        for oh in range(2):
            pb = P[oh]
            for it in range(NI0):
                em.pe(lambda e, pb=pb, it=it, oh=oh, hq=hq_pair[c % 2]:
                      e.matmul(
                    pb[:, :], hq[:, 128 * it:128 * (it + 1)],
                    wdT_q[:, DM * it + 512 * oh: DM * it + 512 * (oh + 1)],
                    start=(it == 0), stop=(it == NI0 - 1)))
            pen = em.now("pe")
            em.wait("vector", "pe", pen)
            xs = x_sb[:, c * DM + 512 * oh: c * DM + 512 * (oh + 1)]
            em.v(lambda e, xs=xs, pb=pb: e.scalar_tensor_tensor(
                xs, pb[:, :], sh_b[:, :], xs, OP.mult, OP.add))
    l0_done_v = em.now("v")

    if dbg == "l0" or 1 not in layers:
        for c in range(NCH):
            em.wait("scalar", "v", l0_done_v)
            em.dma("scalar", d["dbg0" if dbg else "out_x"].ap()
                   [128 * c:128 * (c + 1), :],
                   x_sb[:, c * DM:(c + 1) * DM], "sd")
        em.wait("scalar", "sd", em.now("sd"))
        em.play()
        ctx.close()
        return nc, d

    HD = 64
    rope_sb = nc.alloc_sbuf_tensor_at("rope_sb", [128, 64], F32,
                                      offset=SBUF_BASE + 212566 + 10)

    for c in range(NCH):
        ln_stats(c, m1, f1, nb1)
    em.wait("scalar", "v", em.now("v"))
    em.sc(lambda e: e.activation(b1q8[:, :], b1q[:, :], AF.Copy))
    em.sc(lambda e: e.activation(b1q[:, :], b1q8[:, :], AF.Copy))

    AQ = Arena(nc, PHASE_BASE - SBUF_BASE)
    a_sb = AQ.take("a_sb", [128, NCH * DM], F32)
    AEND = (AQ.off + 63) & ~63
    for c in range(NCH):
        xc = x_sb[:, c * DM:(c + 1) * DM]
        ac = a_sb[:, c * DM:(c + 1) * DM]
        em.wait("vector", "pe", em.now("pe"))
        em.v(lambda e, xc=xc, ac=ac, c=c: e.tensor_scalar(
            ac, xc, m1[:, c:c + 1], f1[:, c:c + 1], OP.subtract, OP.mult))
        vv = em.now("v")
        em.wait("scalar", "v", vv)
        em.wait("scalar", "sd", em.now("sd"))
        em.sc(lambda e, ac=ac: e.activation(t8[:, :], ac, AF.Copy))
        em.wait("vector", "sc", em.now("sc"))
        em.v(lambda e: e.tensor_copy(tbf[:, :], t8[:, :]))
        em.wait("tensor", "v", em.now("v"))
        em.wait("tensor", "sd", em.now("sd"))
        transpose8(tbf, T6)
        em.wait("vector", "pe", em.now("pe"))
        em.v(lambda e: e.tensor_copy(t8[:, :], T6[:, :]))
        em.wait("scalar", "v", em.now("v"))
        dst = ap3(d["xq8_hbm"], c * 128, [[S, 128], [128 * S, 8], [1, 128]])
        em.dma("scalar", dst, ap3(t8, 0, [[DM, 128], [128, 8], [1, 128]]),
               "sd")
        em.dma("scalar", d["x_hbm"].ap()[128 * c:128 * (c + 1), :], xc, "sd")
    xsp_sd = em.now("sd")

    AX_ = Arena(nc, 0, limit=65664)
    kT1 = AX_.take("kT1", [128, 2 * S], BF16)
    v1_sb = AX_.take("v1_sb", [128, 16 * 260], BF16)
    qT1 = AX_.take("qT1", [128, KT * 512], BF16)
    xrT = AX_.take("xrT", [128, KT * S], BF16)
    yT1 = AX_.take("yT1", [128, KT * 512], BF16)
    AU = Arena(nc, AEND - SBUF_BASE)
    gwq_bf = AU.take("gwq_bf", [128, KT * DM], BF16)
    gkvd_bf = AU.take("gkvd_bf", [128, KT * 256], BF16)
    gkup_bf = AU.take("gkup_bf", [128, 2 * 256], BF16)
    gvup_bf = AU.take("gvup_bf", [128, 2 * 256], BF16)
    gwo_bf = AU.take("gwo_bf", [128, KT * DM], BF16)
    latT_c = AU.take("latT_c", [128, 256], BF16)
    PT = [AU.take(f"PT{i}", [128, 512], BF16) for i in range(2)]
    ystg1 = AU.take("ystg1", [64, 512], BF16)

    l0_end_pe, l0_end_v = em.now("pe"), em.now("v")
    em.wait("gpsimd", "pe", l0_end_pe)
    em.wait("gpsimd", "v", l0_end_v)
    for wname, wsb, wid in (("gwq", gwq_bf, DM), ("gkvd", gkvd_bf, 256),
                            ("gkup", gkup_bf, 256), ("gvup", gvup_bf, 256),
                            ("gwo", gwo_bf, DM)):
        nk = 8 if wid == DM or wname == "gkvd" else 2
        nk = {"gwq": 8, "gkvd": 8, "gkup": 2, "gvup": 2, "gwo": 8}[wname]
        src = ap3(d[wname], 0, [[wid, 128], [128 * wid, nk], [1, wid]])
        dstw = ap3(wsb, 0, [[nk * wid, 128], [wid, nk], [1, wid]])
        em.dma("gpsimd", dstw, src, "gw")
    gl_w = em.now("gw")
    em.wait("vector", "gw", gl_w)
    for kt in range(KT):
        em.v(lambda e, kt=kt: e.tensor_scalar(
            gwq_bf[:, DM * kt:DM * (kt + 1)],
            gwq_bf[:, DM * kt:DM * (kt + 1)],
            n1w_sb[:, kt:kt + 1], None, OP.mult))
        em.v(lambda e, kt=kt: e.tensor_scalar(
            gkvd_bf[:, 256 * kt:256 * (kt + 1)],
            gkvd_bf[:, 256 * kt:256 * (kt + 1)],
            n1w_sb[:, kt:kt + 1], None, OP.mult))
    gw_fold_v = em.now("v")

    def rope_norm(width, nheads, fcol):
        em.wait("scalar", "v", em.now("v"))
        em.sc(lambda e: e.activation(sB[:, 0:width], sA[:, 0:width],
                                     AF.Square))
        em.wait("vector", "sc", em.now("sc"))
        em.v(lambda e: e.tensor_reduce(
            hdf[:, fcol:fcol + nheads],
            ap3(sB, 0, [[DM, 128], [64, nheads], [1, 64]]), AX, OP.add))
        em.v(lambda e: e.tensor_scalar(
            hdf[:, fcol:fcol + nheads], hdf[:, fcol:fcol + nheads],
            1.0 / HD, 1e-6, OP.mult, OP.add))
        em.v(lambda e: e.reciprocal(hdf[:, fcol:fcol + nheads],
                                    hdf[:, fcol:fcol + nheads]))
        em.wait("scalar", "v", em.now("v"))
        em.sc(lambda e: e.activation(hdf[:, fcol:fcol + nheads],
                                     hdf[:, fcol:fcol + nheads], AF.Sqrt))
        em.wait("vector", "sc", em.now("sc"))
        ev = ap3(sA, 0, [[DM, 128], [64, nheads], [2, 32]])
        od = ap3(sA, 1, [[DM, 128], [64, nheads], [2, 32]])
        rev = ap3(sB, 0, [[DM, 128], [64, nheads], [2, 32]])
        rod = ap3(sB, 1, [[DM, 128], [64, nheads], [2, 32]])
        cosb = ap3(rope_sb, 0, [[64, 128], [0, nheads], [1, 32]])
        sinb = ap3(rope_sb, 32, [[64, 128], [0, nheads], [1, 32]])
        em.v(lambda e: e.tensor_tensor(rev, ev, cosb, OP.mult))
        em.v(lambda e: e.tensor_tensor(rod, ev, sinb, OP.mult))
        em.v(lambda e: e.tensor_tensor(ev, od, sinb, OP.mult))
        em.v(lambda e: e.tensor_tensor(rev, rev, ev, OP.subtract))
        em.v(lambda e: e.tensor_tensor(ev, od, cosb, OP.mult))
        em.v(lambda e: e.tensor_tensor(rod, rod, ev, OP.add))
        fb = ap3(hdf, fcol, [[16, 128], [1, nheads], [0, 64]])
        em.v(lambda e: e.tensor_tensor(
            ap3(tbf, 0, [[DM, 128], [64, nheads], [1, 64]]),
            ap3(sB, 0, [[DM, 128], [64, nheads], [1, 64]]), fb, OP.mult))
        return em.now("v")

    def l1_attn(prefix, tables, kvd_mode, wq_sb, wo_sb, wk_sb=None,
                wv_sb=None, fr_sb=None):
        rope_v = 0
        ystg1_dma = [0]
        last_exp_sc = {0: 0, 1: 0}
        bank_free_v = {0: 0, 1: 0}
        sc_base = [0]
        for c in range(NCH):
            ac = a_sb[:, c * DM:(c + 1) * DM]
            em.wait("sync", "v", rope_v)
            em.dma("sync", rope_sb[:, 0:32],
                   d[tables].ap()[:, 32 * c:32 * c + 32], "ld")
            rtl = em.dma("sync", rope_sb[:, 32:64],
                         d[tables].ap()[:, 512 + 32 * c:512 + 32 * c + 32],
                         "ld")
            em.wait("scalar", "v", em.now("v"))
            em.sc(lambda e, ac=ac, c=c: e.activation(
                sA[:, :], ac, AF.Square, accum_out=ss_t[:, c:c + 1]))
            em.wait("vector", "sc", em.now("sc"))
            em.v(lambda e, c=c: e.tensor_scalar(
                sB[:, 0:1], ss_t[:, c:c + 1], 1.0 / DM, 1e-6, OP.mult,
                OP.add))
            em.v(lambda e: e.reciprocal(sB[:, 1:2], sB[:, 0:1]))
            em.wait("scalar", "v", em.now("v"))
            em.sc(lambda e, c=c: e.activation(fr1[:, c:c + 1], sB[:, 1:2],
                                              AF.Sqrt))
            em.wait("vector", "sc", em.now("sc"))
            em.wait("vector", "pe", em.now("pe"))
            em.v(lambda e, ac=ac, c=c: e.tensor_scalar(
                tbf[:, :], ac, fr1[:, c:c + 1], None, OP.mult))
            em.wait("tensor", "v", em.now("v"))
            transpose8(tbf, T6)
            em.wait("vector", "pe", em.now("pe"))
            em.v(lambda e, c=c: e.tensor_copy(
                ap3(xrT, 128 * c, [[KT * S, 128], [S, 8], [1, 128]]),
                ap3(T6, 0, [[1024, 128], [128, 8], [1, 128]])))
            em.wait("tensor", "v", em.now("v"))
            em.wait("tensor", "gw", em.now("gw"))
            if kvd_mode:
                for lt in range(2):
                    for kt in range(KT):
                        em.pe(lambda e, lt=lt, kt=kt, c=c: e.matmul(
                            P[lt][:, 0:128],
                            gkvd_bf[:, 256 * kt + 128 * lt:
                                    256 * kt + 128 * (lt + 1)],
                            xrT[:, S * kt + 128 * c:S * kt + 128 * (c + 1)],
                            start=(kt == 0), stop=(kt == KT - 1)))
                em.wait("vector", "pe", em.now("pe"))
                for lt in range(2):
                    em.v(lambda e, lt=lt: e.tensor_copy(
                        latT_c[:, 128 * lt:128 * (lt + 1)], P[lt][:, 0:128]))
                em.wait("tensor", "v", em.now("v"))
                for lt in range(2):
                    em.pe(lambda e, lt=lt: e.matmul(
                        P[2][:, 0:256], latT_c[:, 128 * lt:128 * (lt + 1)],
                        gkup_bf[:, 256 * lt:256 * (lt + 1)],
                        start=(lt == 0), stop=(lt == 1)))
                for lt in range(2):
                    em.pe(lambda e, lt=lt: e.matmul(
                        P[3][:, 0:256], latT_c[:, 128 * lt:128 * (lt + 1)],
                        gvup_bf[:, 256 * lt:256 * (lt + 1)],
                        start=(lt == 0), stop=(lt == 1)))
            else:
                for kt in range(KT):
                    em.pe(lambda e, kt=kt, c=c: e.matmul(
                        P[2][:, 0:256],
                        xrT[:, S * kt + 128 * c:S * kt + 128 * (c + 1)],
                        wk_sb[:, 256 * kt:256 * (kt + 1)],
                        start=(kt == 0), stop=(kt == KT - 1)))
                for kt in range(KT):
                    em.pe(lambda e, kt=kt, c=c: e.matmul(
                        P[3][:, 0:256],
                        xrT[:, S * kt + 128 * c:S * kt + 128 * (c + 1)],
                        wv_sb[:, 256 * kt:256 * (kt + 1)],
                        start=(kt == 0), stop=(kt == KT - 1)))
            pen = em.now("pe")
            em.wait("scalar", "pe", pen)
            em.sc(lambda e: e.activation(sA[:, 0:256], P[2][:, 0:256],
                                         AF.Copy))
            em.wait("vector", "pe", pen)
            em.v(lambda e, c=c: e.tensor_copy(
                ap3(v1_sb, 260 * c, [[16 * 260, 128], [65, 4], [1, 64]]),
                ap3(P[3], 0, [[512, 128], [64, 4], [1, 64]])))
            em.wait("vector", "ld", rtl)
            em.wait("vector", "sc", em.now("sc"))
            rope_norm(256, 4, 0)
            rope_v = em.now("v")
            em.wait("tensor", "v", rope_v)
            for lt in range(2):
                em.pe(lambda e, lt=lt: e.transpose(
                    T7[:, 128 * lt:128 * (lt + 1)],
                    tbf[:, 128 * lt:128 * (lt + 1)], ident[:, :]))
            em.wait("vector", "pe", em.now("pe"))
            em.v(lambda e, c=c: e.tensor_copy(
                ap3(kT1, 128 * c, [[2 * S, 128], [S, 2], [1, 128]]),
                ap3(T7, 0, [[1024, 128], [128, 2], [1, 128]])))
        prep_v = em.now("v")

        for g in range(NG):
            for cl in range(4):
                c = 4 * g + cl
                em.wait("sync", "v", rope_v)
                em.dma("sync", rope_sb[:, 0:32],
                       d[tables].ap()[:, 32 * c:32 * c + 32], "ld")
                rtl = em.dma("sync", rope_sb[:, 32:64],
                             d[tables].ap()[:, 512 + 32 * c:512 + 32 * c + 32],
                             "ld")
                em.wait("tensor", "sc", em.now("sc"))
                em.wait("tensor", "v", em.now("v"))
                for oh in range(2):
                    for kt in range(KT):
                        em.pe(lambda e, oh=oh, kt=kt, c=c: e.matmul(
                            P[oh][:, :],
                            xrT[:, S * kt + 128 * c:S * kt + 128 * (c + 1)],
                            wq_sb[:, DM * kt + 512 * oh:
                                  DM * kt + 512 * (oh + 1)],
                            start=(kt == 0), stop=(kt == KT - 1)))
                pen = em.now("pe")
                em.wait("scalar", "pe", pen)
                for oh in range(2):
                    em.sc(lambda e, oh=oh: e.activation(
                        sA[:, 512 * oh:512 * (oh + 1)], P[oh][:, :], AF.Copy))
                em.wait("vector", "ld", rtl)
                em.wait("vector", "sc", em.now("sc"))
                rope_norm(DM, 16, 0)
                rope_v = em.now("v")
                em.wait("tensor", "v", rope_v)
                transpose8(tbf, T6)
                em.wait("vector", "pe", em.now("pe"))
                em.v(lambda e, cl=cl: e.tensor_copy(
                    ap3(qT1, 128 * cl, [[KT * 512, 128], [512, 8], [1, 128]]),
                    ap3(T6, 0, [[1024, 128], [128, 8], [1, 128]])))
            qprep_v = em.now("v")
            sc_base[0] = em.now("sc")

            if prefix == "g":
                jbs = list(range(4 * g + 4))
            else:
                jbs = list(range(max(0, 4 * g - 2), 4 * g + 4))

            pend_norm = [None]

            def emit_norm():
                if pend_norm[0] is None:
                    return
                hh, bank, pe_cnt = pend_norm[0]
                pend_norm[0] = None
                em.wait("vector", "pe", pe_cnt)
                em.v(lambda e, bank=bank: e.reciprocal(
                    recd[:, :], bank[64:65, :]))
                em.wait("tensor", "v", em.now("v"))
                em.pe(lambda e: e.matmul(P[4][0:64, :],
                                         ones_row_f[0:1, 0:64], recd[:, :],
                                         start=True, stop=True))
                em.wait("vector", "pe", em.now("pe"))
                em.v(lambda e: e.tensor_copy(recb[0:64, :], P[4][0:64, :]))
                if hh % 2 == 0:
                    dsty = yT1[0:64, 512 * (hh // 2):512 * (hh // 2) + 512]
                else:
                    dsty = ystg1[0:64, 0:512]
                    em.wait("vector", "ld", ystg1_dma[0])
                em.v(lambda e, dsty=dsty, bank=bank: e.tensor_tensor(
                    dsty, bank[0:64, :], recb[0:64, :], OP.mult))
                bank_free_v[hh % 2] = em.now("v")
                if hh % 2 == 1:
                    em.wait("sync", "v", em.now("v"))
                    ystg1_dma[0] = em.dma(
                        "sync", yT1[64:128, 512 * (hh // 2):
                                    512 * (hh // 2) + 512],
                        ystg1[0:64, :], "ld")

            for h in range(16):
                kv = (0 if h < 8 else 2) + (h % 2)
                ktile, pofk = kv // 2, 64 * (kv % 2)
                pofq = 64 * (h % 2)
                ot_bank = P[2 + h % 2]
                def jgeo(ji):
                    jb = jbs[ji]
                    jloc = jb - 4 * g
                    if prefix == "g":
                        istart = 128 * jloc if jloc > 0 else 0
                        iend = 512
                    else:
                        istart = 128 * max(0, jloc)
                        iend = 128 * (min(3, jloc + 2) + 1)
                    return jb, jloc, istart, iend - istart

                def emit_score(ji):
                    jb, jloc, istart, W = jgeo(ji)
                    pi = ji % 2
                    em.wait("tensor", "sc", max(last_exp_sc[pi], sc_base[0]))
                    em.wait("tensor", "v", qprep_v)
                    em.pe(lambda e, pi=pi, W=W, jb=jb, ktile=ktile,
                          pofk=pofk, pofq=pofq, h=h, istart=istart: e.matmul(
                        P[pi][:, 0:W],
                        kT1[pofk:pofk + 64,
                            S * ktile + 128 * jb:S * ktile + 128 * (jb + 1)],
                        qT1[pofq:pofq + 64,
                            512 * (h // 2) + istart:
                            512 * (h // 2) + istart + W],
                        start=True, stop=True))

                if prefix == "l":
                    em.wait("tensor", "v", bank_free_v[h % 2])
                    em.pe(lambda e, ot_bank=ot_bank: e.matmul(
                        ot_bank[0:65, 0:512], ones_row_bf[0:1, 0:65],
                        zrow_bf[0:1, :], start=True, stop=False))
                emit_score(0)
                emit_norm()
                for ji, jb in enumerate(jbs):
                    jb, jloc, istart, W = jgeo(ji)
                    pi = ji % 2
                    em.wait("scalar", "pe", em.now("pe"))
                    em.sc(lambda e, pi=pi, W=W, pt=PT[pi % len(PT)]:
                          e.activation(
                        pt[:, 0:W], P[pi][:, 0:W], AF.Exp,
                        bias=neg8_b[:, :], scale=0.125))
                    scn = em.now("sc")
                    last_exp_sc[pi] = scn
                    need_v = False
                    if 0 <= jloc <= 3:
                        em.wait("vector", "sc", scn)
                        dcol = 128 * jloc - istart
                        em.v(lambda e, pi=pi, dcol=dcol,
                             pt=PT[pi % len(PT)]: e.tensor_tensor(
                            pt[:, dcol:dcol + 128],
                            pt[:, dcol:dcol + 128],
                            causal4[:, 0:128], OP.mult))
                        need_v = True
                    if prefix == "l" and jloc + 2 <= 3:
                        em.wait("vector", "sc", scn)
                        wcol = 128 * (jloc + 2) - istart
                        em.v(lambda e, pi=pi, wcol=wcol,
                             pt=PT[pi % len(PT)]: e.tensor_tensor(
                            pt[:, wcol:wcol + 128],
                            pt[:, wcol:wcol + 128],
                            winm[:, :], OP.mult))
                        need_v = True
                    vmask = em.now("v")
                    if ji + 1 < len(jbs):
                        emit_score(ji + 1)
                    if need_v:
                        em.wait("tensor", "v", vmask)
                    else:
                        em.wait("tensor", "sc", scn)
                    if ji == 0:
                        em.wait("tensor", "v", bank_free_v[h % 2])
                    em.pe(lambda e, jb=jb, kv=kv, W=W, istart=istart,
                          pi=pi, ji=ji, last=(ji == len(jbs) - 1),
                          st=(ji == 0 and prefix != "l"),
                          pt=PT[pi % len(PT)],
                          ot_bank=ot_bank: e.matmul(
                        ot_bank[0:65, istart:istart + W],
                        v1_sb[:, 260 * jb + 65 * kv:260 * jb + 65 * kv + 65],
                        pt[:, 0:W],
                        start=st, stop=last))
                pend_norm[0] = (h, ot_bank, em.now("pe"))
            emit_norm()
            em.wait("tensor", "ld", em.now("ld"))
            em.wait("tensor", "v", em.now("v"))
            for cl in range(4):
                c = 4 * g + cl
                for oh in range(2):
                    pb = P[oh]
                    em.wait("tensor", "v", em.now("v"))
                    for kt in range(KT):
                        em.pe(lambda e, pb=pb, kt=kt, cl=cl, oh=oh: e.matmul(
                            pb[:, :],
                            yT1[:, 512 * kt + 128 * cl:
                                512 * kt + 128 * (cl + 1)],
                            wo_sb[:, DM * kt + 512 * oh:
                                  DM * kt + 512 * (oh + 1)],
                            start=(kt == 0), stop=(kt == KT - 1)))
                    pen = em.now("pe")
                    em.wait("vector", "pe", pen)
                    aslice = a_sb[:, c * DM + 512 * oh:
                                  c * DM + 512 * (oh + 1)]
                    em.v(lambda e, aslice=aslice, pb=pb: e.tensor_add(
                        aslice, aslice, pb[:, :]))

    em.wait("vector", "sd", xsp_sd)
    em.v(lambda e: e.memset(
        ap3(v1_sb, 64, [[16 * 260, 128], [260, 16], [65, 4]]), 1.0))
    em.wait("vector", "v", gw_fold_v)
    if "g" in L1_PARTS:
        l1_attn("g", "ropes_g", True, gwq_bf, gwo_bf)

    AU2 = Arena(nc, AEND - SBUF_BASE)
    lwq_bf = AU2.take("lwq_bf", [128, KT * DM], BF16)
    lwk_bf = AU2.take("lwk_bf", [128, KT * 256], BF16)
    lwv_bf = AU2.take("lwv_bf", [128, KT * 256], BF16)
    lwo_bf = AU2.take("lwo_bf", [128, KT * DM], BF16)
    PT2 = [AU2.take(f"PT2{i}", [128, 512], BF16) for i in range(2)]
    ystg2 = AU2.take("ystg2", [64, 512], BF16)
    glob_pe, glob_v, glob_sc = em.now("pe"), em.now("v"), em.now("sc")
    em.wait("gpsimd", "pe", glob_pe)
    em.wait("gpsimd", "v", glob_v)
    em.wait("gpsimd", "sc", glob_sc)
    for wname, wsb, wid, nk in (("lwq", lwq_bf, DM, 8), ("lwk", lwk_bf, 256, 8),
                                ("lwv", lwv_bf, 256, 8), ("lwo", lwo_bf, DM, 8)):
        src = ap3(d[wname], 0, [[wid, 128], [128 * wid, nk], [1, wid]])
        dstw = ap3(wsb, 0, [[nk * wid, 128], [wid, nk], [1, wid]])
        em.dma("gpsimd", dstw, src, "gw")
    loc_w = em.now("gw")
    em.wait("vector", "gw", loc_w)
    for kt in range(KT):
        em.v(lambda e, kt=kt: e.tensor_scalar(
            lwq_bf[:, DM * kt:DM * (kt + 1)],
            lwq_bf[:, DM * kt:DM * (kt + 1)],
            n2w_sb[:, kt:kt + 1], None, OP.mult))
        em.v(lambda e, kt=kt: e.tensor_scalar(
            lwk_bf[:, 256 * kt:256 * (kt + 1)],
            lwk_bf[:, 256 * kt:256 * (kt + 1)],
            n2w_sb[:, kt:kt + 1], None, OP.mult))
        em.v(lambda e, kt=kt: e.tensor_scalar(
            lwv_bf[:, 256 * kt:256 * (kt + 1)],
            lwv_bf[:, 256 * kt:256 * (kt + 1)],
            n2w_sb[:, kt:kt + 1], None, OP.mult))
    PT_save = PT[:]
    PT[0], PT[1] = PT2[0], PT2[1]
    ystg1_save = ystg1
    ystg1 = ystg2
    if "l" in L1_PARTS:
        l1_attn("l", "ropes_l", False, lwq_bf, lwo_bf, wk_sb=lwk_bf,
                wv_sb=lwv_bf)
    PT[0], PT[1] = PT_save[0], PT_save[1]
    ystg1 = ystg1_save

    ARf = Arena(nc, 0, limit=65664)
    w1q = ARf.take("w1q", [128, KT * H1], FP8)
    w2q = ARf.take("w2q", [128, NH1 * DM], FP8)
    AF_ = Arena(nc, AEND - SBUF_BASE)
    xq8tb = [AF_.take(f"xq8tb{i}", [128, KT * 512], FP8) for i in range(2)]
    h8_g = AF_.take("h8_g", [128, NH1 * 512], FP8)
    h16b = [AF_.take(f"h16b{i}", [128, 512], FP16) for i in range(2)]
    y16b = [AF_.take(f"y16b{i}", [128, 512], FP16) for i in range(2)]
    fin = [AF_.take(f"fin{i}", [128, DM], F32) for i in range(2)]
    b2row8 = AF_.take("b2row8", [1, DM], FP8)
    b2row_bf = AF_.take("b2row_bf", [1, DM], BF16)
    b2bc = AF_.take("b2bc", [128, DM], F32)
    dropb = [AF_.take(f"dropb{i}", [128, DM], BF16) for i in range(2)]

    if "ffn" in L1_PARTS:
        loc_pe, loc_v, loc_sc = em.now("pe"), em.now("v"), em.now("sc")
        em.wait("gpsimd", "pe", loc_pe)
        em.wait("gpsimd", "v", loc_v)
        em.wait("gpsimd", "sc", loc_sc)
        em.wait("sync", "pe", loc_pe)
        em.wait("sync", "v", loc_v)
        b2_ld = em.dma("sync", rowstg[:, :], d["b2_row"].ap(), "ld")
        em.wait("scalar", "ld", b2_ld)
        em.wait("scalar", "v", loc_v)
        em.sc(lambda e: e.activation(b2row8[:, :], rowstg[:, :], AF.Copy))
        em.wait("vector", "sc", em.now("sc"))
        em.v(lambda e: e.tensor_copy(b2row_bf[:, :], b2row8[:, :]))
        em.wait("tensor", "v", em.now("v"))
        for oh in range(2):
            em.pe(lambda e, oh=oh: e.matmul(
                P[oh][:, :], ones_row_bf[:, :],
                b2row_bf[0:1, 512 * oh:512 * (oh + 1)], start=True, stop=True))
        em.wait("vector", "pe", em.now("pe"))
        for oh in range(2):
            em.v(lambda e, oh=oh: e.tensor_copy(
                b2bc[:, 512 * oh:512 * (oh + 1)], P[oh][:, :]))
        src = ap3(d["w1_t"], 0, [[H1, 128], [128 * H1, KT], [1, H1]])
        em.dma("gpsimd", ap3(w1q, 0, [[KT * H1, 128], [H1, KT], [1, H1]]), src,
               "gw")
        src = ap3(d["w2_t"], 0, [[DM, 128], [128 * DM, NH1], [1, DM]])
        em.dma("gpsimd", ap3(w2q, 0, [[NH1 * DM, 128], [DM, NH1], [1, DM]]), src,
               "gw")
        f_w = em.now("gw")
        xq_cons = [0, 0]
        drop_cons = [0, 0]
        ffn_cons_sc = {0: em.now("sc"), 1: em.now("sc")}
        ffn_v_guard = em.now("v")
        for g in range(NG):
            em.wait("sync", "pe", xq_cons[g % 2])
            srcx = ap3(d["xq8_hbm"], 512 * g,
                       [[S, 128], [128 * S, KT], [1, 512]])
            xql = em.dma("sync", ap3(xq8tb[g % 2], 0,
                                     [[KT * 512, 128], [512, KT], [1, 512]]),
                         srcx, "ld")
            em.wait("tensor", "ld", xql)
            em.wait("tensor", "gw", f_w)
            for ht in range(NH1):
                pb = P[ht % 2]
                em.wait("tensor", "sc", ffn_cons_sc[ht % 2])
                em.wait("tensor", "v", ffn_v_guard)
                for kt in range(KT):
                    em.pe(lambda e, pb=pb, ht=ht, kt=kt, g=g: e.matmul(
                        pb[:, :],
                        w1q[:, H1 * kt + 128 * ht:H1 * kt + 128 * (ht + 1)],
                        xq8tb[g % 2][:, 512 * kt:512 * (kt + 1)],
                        start=(kt == 0), stop=(kt == KT - 1)))
                pen = em.now("pe")
                em.wait("scalar", "pe", pen)
                em.wait("scalar", "v", em.now("v"))
                em.sc(lambda e, pb=pb, ht=ht: e.activation(
                    h16b[ht % 2][:, :], pb[:, :], AF.Identity,
                    bias=b1q[:, ht:ht + 1]))
                ffn_cons_sc[ht % 2] = em.now("sc")
                em.wait("vector", "sc", em.now("sc"))
                em.v(lambda e, ht=ht: e.tensor_scalar_max(
                    h8_g[:, 512 * ht:512 * (ht + 1)], h16b[ht % 2][:, :], 0.0))
            xq_cons[g % 2] = em.now("pe")
            for cl in range(4):
                c = 4 * g + cl
                em.wait("gpsimd", "v", drop_cons[c % 2])
                drl = em.dma("gpsimd", dropb[c % 2][:, :],
                             d["drop_c"].ap()[128 * c:128 * (c + 1), :], "gw")
                for oh in range(2):
                    pb = P[2 + oh]
                    em.wait("tensor", "v", em.now("v"))
                    for ht in range(NH1):
                        em.pe(lambda e, pb=pb, ht=ht, cl=cl, oh=oh: e.matmul(
                            pb[:, :],
                            h8_g[:, 512 * ht + 128 * cl:
                                 512 * ht + 128 * (cl + 1)],
                            w2q[:, DM * ht + 512 * oh:DM * ht + 512 * (oh + 1)],
                            start=(ht == 0), stop=(ht == NH1 - 1)))
                    pen = em.now("pe")
                    em.wait("vector", "pe", pen)
                    em.wait("vector", "gw", drl)
                    em.v(lambda e, pb=pb, oh=oh: e.tensor_tensor(
                        y16b[oh][:, :], pb[:, :],
                        b2bc[:, 512 * oh:512 * (oh + 1)], OP.add))
                    em.v(lambda e, oh=oh, c=c: e.tensor_tensor(
                        sA[:, 512 * oh:512 * (oh + 1)], y16b[oh][:, :],
                        dropb[c % 2][:, 512 * oh:512 * (oh + 1)], OP.mult))
                    aslice = a_sb[:, c * DM + 512 * oh:c * DM + 512 * (oh + 1)]
                    em.v(lambda e, aslice=aslice, oh=oh: e.tensor_add(
                        aslice, aslice, sA[:, 512 * oh:512 * (oh + 1)]))
                drop_cons[c % 2] = em.now("v")
        ffn_v = em.now("v")

    em.wait("sync", "pe", em.now("pe"))
    em.wait("sync", "v", em.now("v"))
    em.wait("sync", "sc", em.now("sc"))
    fin_cons = [0, 0]
    for c in range(NCH):
        em.wait("sync", "v", fin_cons[c % 2])
        fl = em.dma("sync", fin[c % 2][:, :],
                    d["x_hbm"].ap()[128 * c:128 * (c + 1), :], "ld")
        em.wait("vector", "ld", fl)
        em.wait("vector", "sd", em.now("sd"))
        em.v(lambda e, c=c: e.tensor_add(
            sB[:, :], fin[c % 2][:, :], a_sb[:, c * DM:(c + 1) * DM]))
        fin_cons[c % 2] = em.now("v")
        em.wait("scalar", "v", em.now("v"))
        em.dma("scalar", d["out_x"].ap()[128 * c:128 * (c + 1), :],
               sB[:, :], "sd")
    em.wait("scalar", "sd", em.now("sd"))

    em.play()
    ctx.close()
    return nc, d



def _host_consts():
    bf = ml_dtypes.bfloat16
    ident = np.eye(128, dtype=bf)
    j = np.arange(128)[:, None]
    i = np.arange(128)[None, :]
    causal = (j <= i).astype(np.float32)
    causal4 = np.ascontiguousarray(np.tile(causal, (1, 4))).astype(bf)
    win = (j > i).astype(np.float32).astype(bf)
    ropes = {}
    for name, base in (("ropes_g", 1e6), ("ropes_l", 1e4)):
        inv = 1.0 / (base ** (np.arange(0, 64, 2, dtype=np.float64) / 64.0))
        pos = np.arange(S, dtype=np.float64)
        fr = pos[:, None] * inv[None, :]
        cos = np.cos(fr).astype(np.float32).reshape(16, 128, 32)
        sin = np.sin(fr).astype(np.float32).reshape(16, 128, 32)
        cos = np.ascontiguousarray(cos.transpose(1, 0, 2).reshape(128, 512))
        sin = np.ascontiguousarray(sin.transpose(1, 0, 2).reshape(128, 512))
        ropes[name] = np.ascontiguousarray(
            np.concatenate([cos, sin], axis=1))
    return ident, causal4, win, ropes


def _rotl(x, r):
    return ((x << np.uint32(r)) | (x >> np.uint32(32 - r))).astype(np.uint32)


def _threefry2x32(k0, k1, x0, x1):
    ks0 = np.uint32(k0); ks1 = np.uint32(k1)
    ks2 = np.uint32(ks0 ^ ks1 ^ np.uint32(0x1BD11BDA))
    x0 = (x0 + ks0).astype(np.uint32); x1 = (x1 + ks1).astype(np.uint32)
    R0 = (13, 15, 26, 6); R1 = (17, 29, 16, 24)

    def rounds(x0, x1, rots):
        for r in rots:
            x0 = (x0 + x1).astype(np.uint32)
            x1 = _rotl(x1, r)
            x1 = (x1 ^ x0).astype(np.uint32)
        return x0, x1
    x0, x1 = rounds(x0, x1, R0)
    x0 = (x0 + ks1).astype(np.uint32); x1 = (x1 + ks2 + np.uint32(1)).astype(np.uint32)
    x0, x1 = rounds(x0, x1, R1)
    x0 = (x0 + ks2).astype(np.uint32); x1 = (x1 + ks0 + np.uint32(2)).astype(np.uint32)
    x0, x1 = rounds(x0, x1, R0)
    x0 = (x0 + ks0).astype(np.uint32); x1 = (x1 + ks1 + np.uint32(3)).astype(np.uint32)
    x0, x1 = rounds(x0, x1, R1)
    x0 = (x0 + ks1).astype(np.uint32); x1 = (x1 + ks2 + np.uint32(4)).astype(np.uint32)
    x0, x1 = rounds(x0, x1, R0)
    x0 = (x0 + ks2).astype(np.uint32); x1 = (x1 + ks0 + np.uint32(5)).astype(np.uint32)
    return x0, x1


def _host_drop():
    n = 8 * S * DM
    i = np.arange(n, dtype=np.uint64)
    c1 = (i >> np.uint64(32)).astype(np.uint32)
    c2 = (i & np.uint64(0xFFFFFFFF)).astype(np.uint32)
    b0, b1 = _threefry2x32(np.uint32(0), np.uint32(42), c1, c2)
    import os
    if os.environ.get("BASS_SIM_FP64") == "1":
        bits = (b0.astype(np.uint64) << np.uint64(32)) | b1.astype(np.uint64)
        u = ((bits >> np.uint64(12)) | np.uint64(0x3FF0000000000000)).view(np.float64) - 1.0
        keep = u < 0.8
    else:
        bits = (b0 ^ b1).astype(np.uint32)
        u = ((bits >> np.uint32(9)) | np.uint32(0x3F800000)).view(np.float32) - np.float32(1.0)
        keep = u < np.float32(0.8)
    keep = keep.reshape(8, S, DM)
    return np.where(keep, np.float32(1.25),
                    np.float32(0.0)).astype(ml_dtypes.bfloat16)


def kernel(**inputs):
    f32 = np.float32

    def T(a):
        return np.ascontiguousarray(np.asarray(a, f32).T)

    def col(a, n):
        return np.ascontiguousarray(np.asarray(a, f32).reshape(n, 128).T)

    def row(a):
        return np.ascontiguousarray(np.asarray(a, f32).reshape(1, -1))

    ident, causal4, win, ropes = _host_consts()
    drop = _host_drop()
    x = np.asarray(inputs["L0_attn_x"], f32)

    head_of_slot = []
    ev = [0, 1, 2, 3, 8, 9, 10, 11]
    odd = [4, 5, 6, 7, 12, 13, 14, 15]
    for i_ in range(8):
        head_of_slot.append(ev[i_])
        head_of_slot.append(odd[i_])
    qperm = np.concatenate([np.arange(64) + 64 * hh for hh in head_of_slot])

    def permq(w):
        w = np.asarray(w, f32)
        return np.ascontiguousarray(w[:, qperm])

    def permo(w):
        w = np.asarray(w, f32)
        return np.ascontiguousarray(w[qperm, :])

    shared = {
        "wq_t": T(inputs["L0_attn_Wq"]), "wk_t": T(inputs["L0_attn_Wk"]),
        "wv_t": T(inputs["L0_attn_Wv"]), "wo_t": T(inputs["L0_attn_Wo"]),
        "bq": col(inputs["L0_attn_bq"], 8), "bk": col(inputs["L0_attn_bk"], 8),
        "bv_row": row(inputs["L0_attn_bv"]), "bo_row": row(inputs["L0_attn_bo"]),
        "clip_x": np.asarray(inputs["L0_ffn_clip_x"], f32).reshape(1, 1),
        "clip_down": np.asarray(inputs["L0_ffn_clip_down"], f32).reshape(1, 1),
        "wg": np.ascontiguousarray(np.asarray(inputs["L0_ffn_w_gate"], f32)),
        "wu": np.ascontiguousarray(np.asarray(inputs["L0_ffn_w_up"], f32)),
        "wd": np.ascontiguousarray(np.asarray(inputs["L0_ffn_w_down"], f32)),
        "n1w": col(inputs["L1_attn_norm1_w"], 8),
        "n2w": col(inputs["L1_attn_norm2_w"], 8),
        "gwq": permq(inputs["L1_attn_g_wq"]),
        "gkvd": np.ascontiguousarray(np.asarray(inputs["L1_attn_g_kv_down"], f32)),
        "gkup": np.ascontiguousarray(np.asarray(inputs["L1_attn_g_k_up"], f32)),
        "gvup": np.ascontiguousarray(np.asarray(inputs["L1_attn_g_v_up"], f32)),
        "gwo": permo(inputs["L1_attn_g_wo"]),
        "lwq": permq(inputs["L1_attn_l_wq"]),
        "lwk": np.ascontiguousarray(np.asarray(inputs["L1_attn_l_wk"], f32)),
        "lwv": np.ascontiguousarray(np.asarray(inputs["L1_attn_l_wv"], f32)),
        "lwo": permo(inputs["L1_attn_l_wo"]),
        "w1_t": T(inputs["L1_ffn_w1"]), "b1": col(inputs["L1_ffn_b1"], 32),
        "w2_t": T(inputs["L1_ffn_w2"]), "b2_row": row(inputs["L1_ffn_b2"]),
        "ident_c": ident, "causal4_c": causal4, "win_c": win,
        "ropes_g": ropes["ropes_g"], "ropes_l": ropes["ropes_l"],
    }
    in_maps = []
    for i in range(NCORES):
        m = dict(shared)
        m["x_in"] = np.ascontiguousarray(x[i])
        m["drop_c"] = np.ascontiguousarray(drop[i])
        in_maps.append(m)

    nc, _ = build_nc(layers=RUN_LAYERS)
    res = run_bass_kernel_spmd(nc, in_maps, core_ids=list(range(NCORES)))
    out = np.stack([r["out_x"] for r in res.results], axis=0)
    return out.astype(np.float32)


# version 14 (final)
# speedup vs baseline: 1.1264x; 1.0027x over previous
import numpy as np
import ml_dtypes
from contextlib import ExitStack

import concourse.bass as bass
import concourse.mybir as mybir
from concourse.bass_utils import run_bass_kernel_spmd

F32 = mybir.dt.float32
BF16 = mybir.dt.bfloat16
FP16 = mybir.dt.float16
FP8 = mybir.dt.float8e4
AX = mybir.AxisListType.X
AF = mybir.ActivationFunctionType
OP = mybir.AluOpType

MAGIC = float(np.float32(1.5 * 2 ** 23))
S = 2048
DM = 1024
NCH = 16
NG = 4
KT = 8
I0 = 2816
NI0 = 22
H1 = 4096
NH1 = 32
NCORES = 8
RUN_LAYERS = (0, 1)
L1_PARTS = ("g", "l", "ffn")
SBUF_BASE = 16512
SBUF_LIMIT = 212832
ENGS = ("sync", "gpsimd", "vector", "scalar", "tensor")


class Emitter:
    def __init__(self, nc):
        self.nc = nc
        self.q = {e: [] for e in ENGS}
        self.sems = {}
        self.cnt = {}

    def add_sem(self, name, handle):
        self.sems[name] = handle
        self.cnt[name] = 0

    def now(self, sem):
        return self.cnt[sem]

    def wait(self, eng, sem, val):
        if val <= 0:
            return
        assert val <= self.cnt[sem], f"future wait {sem} {val}>{self.cnt[sem]}"
        s = self.sems[sem]
        self.q[eng].append(lambda e, s=s, v=val: e.wait_ge(s, v))

    def op(self, eng, fn, sem=None, amt=1):
        if sem is None:
            self.q[eng].append(fn)
            return None
        s = self.sems[sem]
        prev = self.cnt[sem]
        if prev > 0 and amt == 1:
            self.q[eng].append(lambda e, s=s, v=prev: e.wait_ge(s, v))
        self.cnt[sem] += amt
        self.q[eng].append(lambda e, f=fn, s=s, a=amt: f(e).then_inc(s, a))
        return self.cnt[sem]

    def v(self, fn):
        return self.op("vector", fn, "v")

    def sc(self, fn):
        return self.op("scalar", fn, "sc")

    def pe(self, fn, inc=True):
        return self.op("tensor", fn, "pe" if inc else None)

    def gp(self, fn):
        return self.op("gpsimd", fn, "gp")

    def dma(self, eng, out, in_, sem):
        prev = self.cnt[sem]
        if prev > 0:
            s = self.sems[sem]
            self.q[eng].append(lambda e, s=s, v=prev: e.wait_ge(s, v))
        return self.op(eng, lambda e, o=out, i=in_: e.dma_start(out=o, in_=i),
                       sem, 16)

    def play(self):
        with self.nc.Block() as blk:
            for eng in ENGS:
                thunks = self.q[eng]

                def body(e, th=thunks):
                    for t in th:
                        t(e)
                getattr(blk, eng)(body)


class Arena:
    def __init__(self, nc, base, limit=SBUF_LIMIT):
        self.nc = nc
        self.off = SBUF_BASE + base
        self.limit = SBUF_BASE + limit

    def take(self, name, shape, dtype):
        per = int(np.prod(shape[1:])) * mybir.dt.size(dtype)
        off = (self.off + 31) & ~31
        assert off + per <= self.limit, f"SBUF overflow {name}: {off}+{per}"
        self.off = off + per
        return self.nc.alloc_sbuf_tensor_at(name, list(shape), dtype, offset=off)


def ap3(t, offset, dims):
    return bass.AP(t, offset, dims)


def build_nc(layers=(0, 1), dbg=None):
    nc = bass.Bass("TRN2", target_bir_lowering=False, debug=False)
    d = {}

    def dram(name, shape, dtype, kind="ExternalInput"):
        d[name] = nc.dram_tensor(name, list(shape), dtype, kind=kind)
        return d[name]

    dram("x_in", [S, DM], F32)
    for w in ("wq_t", "wk_t", "wv_t", "wo_t"):
        dram(w, [DM, DM], F32)
    dram("bq", [128, 8], F32)
    dram("bk", [128, 8], F32)
    dram("bv_row", [1, DM], F32)
    dram("bo_row", [1, DM], F32)
    dram("clip_x", [1, 1], F32)
    dram("clip_down", [1, 1], F32)
    dram("wg", [I0, DM], F32)
    dram("wu", [I0, DM], F32)
    dram("wd", [DM, I0], F32)
    dram("n1w", [128, 8], F32)
    dram("n2w", [128, 8], F32)
    dram("gwq", [DM, DM], F32)
    dram("gkvd", [DM, 256], F32)
    dram("gkup", [256, 256], F32)
    dram("gvup", [256, 256], F32)
    dram("gwo", [DM, DM], F32)
    dram("lwq", [DM, DM], F32)
    dram("lwk", [DM, 256], F32)
    dram("lwv", [DM, 256], F32)
    dram("lwo", [DM, DM], F32)
    dram("w1_t", [DM, H1], F32)
    dram("b1", [128, 32], F32)
    dram("w2_t", [H1, DM], F32)
    dram("b2_row", [1, DM], F32)
    dram("ident_c", [128, 128], BF16)
    dram("causal4_c", [128, 512], BF16)
    dram("win_c", [128, 128], BF16)
    dram("ropes_g", [128, 1024], F32)
    dram("ropes_l", [128, 1024], F32)
    dram("drop_c", [S, DM], BF16)
    dram("out_x", [S, DM], F32, kind="ExternalOutput")
    if dbg:
        dram("dbg0", [S, DM], F32, kind="ExternalOutput")
    dram("xq_hbm", [DM, S], BF16, kind="Internal")
    dram("h_hbm", [I0, S], FP16, kind="Internal")
    dram("xq8_hbm", [DM, S], FP8, kind="Internal")
    dram("x_hbm", [S, DM], F32, kind="Internal")
    dram("cc1_in", [1, 128], F32, kind="Internal")
    dram("cc1_out", [1, 128], F32, kind="Internal")
    dram("cc2_in", [1, 128], F32, kind="Internal")
    dram("cc2_out", [1, 128], F32, kind="Internal")

    ctx = ExitStack()
    em = Emitter(nc)
    for sname in ("ld", "gw", "sd", "v", "sc", "pe", "gp", "cc"):
        em.add_sem(sname, ctx.enter_context(nc.semaphore(sname)))

    P = [ctx.enter_context(nc.psum_tensor(f"P{i}", [128, 512], F32))
         for i in range(6)]
    T6 = ctx.enter_context(nc.psum_tensor("T6", [128, 1024], BF16))
    T7 = ctx.enter_context(nc.psum_tensor("T7", [128, 1024], BF16))

    A = Arena(nc, 0)
    x_sb = A.take("x_sb", [128, NCH * DM], F32)
    sA = A.take("sA", [128, DM], F32)
    sB = A.take("sB", [128, DM], F32)
    tbf = A.take("tbf", [128, DM], BF16)
    tbf2 = A.take("tbf2", [128, DM], BF16)
    ystg = A.take("ystg", [64, DM], BF16)
    t8 = A.take("t8", [128, DM], FP8)
    m0 = A.take("m0", [128, 16], F32)
    f0 = A.take("f0", [128, 16], F32)
    nb0 = A.take("nb0", [128, 16], F32)
    sm_t = A.take("sm_t", [128, 16], F32)
    ss_t = A.take("ss_t", [128, 16], F32)
    m1 = A.take("m1", [128, 16], F32)
    f1 = A.take("f1", [128, 16], F32)
    nb1 = A.take("nb1", [128, 16], F32)
    fr1 = A.take("fr1", [128, 16], F32)
    absacc = A.take("absacc", [128, 1], F32)
    absacc2 = A.take("absacc2", [128, 1], F32)
    mxc = A.take("mxc", [128, 1], F32)
    hdf = A.take("hdf", [128, 16], F32)
    recd = A.take("recd", [1, 512], F32)
    mrow = A.take("mrow", [1, 128], F32)
    smt = A.take("smt", [1, 16], F32)
    qsc = A.take("qsc", [128, 1], F32)
    qbi = A.take("qbi", [128, 1], F32)
    rsx_b = A.take("rsx_b", [128, 1], F32)
    sx_b = A.take("sx_b", [128, 1], F32)
    rh_b = A.take("rh_b", [128, 1], F32)
    sh_b = A.take("sh_b", [128, 1], F32)
    ident = A.take("ident", [128, 128], BF16)
    causal4 = A.take("causal4", [128, 512], BF16)
    winm = A.take("winm", [128, 128], BF16)
    onescol = A.take("onescol", [128, 1], BF16)
    ones_row_bf = A.take("ones_row_bf", [1, 128], BF16)
    ones_row_f = A.take("ones_row_f", [1, 128], F32)
    zrow_bf = A.take("zrow_bf", [1, 512], BF16)
    bq_sb = A.take("bq_sb", [128, 8], F32)
    bk_sb = A.take("bk_sb", [128, 8], F32)
    n1w_sb = A.take("n1w_sb", [128, 8], F32)
    n2w_sb = A.take("n2w_sb", [128, 8], F32)
    clipx_sb = A.take("clipx_sb", [1, 1], F32)
    clipd_sb = A.take("clipd_sb", [1, 1], F32)
    rowstg = A.take("rowstg", [1, DM], F32)
    bvrow = A.take("bvrow", [1, DM], BF16)
    borow = A.take("borow", [1, DM], BF16)
    b1q8 = A.take("b1q8", [128, 32], FP8)
    b1q = A.take("b1q", [128, 32], F32)
    magic_b = A.take("magic_b", [128, 1], F32)
    neg8_b = A.take("neg8_b", [128, 1], F32)
    recb = A.take("recb", [64, 512], F32)
    PHASE_BASE = (A.off + 63) & ~63

    em.dma("sync", ident[:, :], d["ident_c"].ap(), "ld")
    em.dma("sync", causal4[:, :], d["causal4_c"].ap(), "ld")
    em.dma("sync", winm[:, :], d["win_c"].ap(), "ld")
    em.dma("sync", bq_sb[:, :], d["bq"].ap(), "ld")
    em.dma("sync", bk_sb[:, :], d["bk"].ap(), "ld")
    em.dma("sync", n1w_sb[:, :], d["n1w"].ap(), "ld")
    em.dma("sync", n2w_sb[:, :], d["n2w"].ap(), "ld")
    em.dma("sync", clipx_sb[:, :], d["clip_x"].ap(), "ld")
    em.dma("sync", clipd_sb[:, :], d["clip_down"].ap(), "ld")
    bv_ld = em.dma("sync", rowstg[:, :], d["bv_row"].ap(), "ld")
    em.dma("sync", b1q[:, :], d["b1"].ap(), "ld")
    ld_consts = em.now("ld")
    x_ld = []
    for c in range(NCH):
        x_ld.append(em.dma("sync", x_sb[:, c * DM:(c + 1) * DM],
                           d["x_in"].ap()[c * 128:(c + 1) * 128, :], "ld"))
    em.wait("vector", "ld", ld_consts)
    em.v(lambda e: e.memset(onescol[:, :], 1.0))
    em.v(lambda e: e.memset(ones_row_bf[:, :], 1.0))
    em.v(lambda e: e.memset(ones_row_f[:, :], 1.0))
    em.v(lambda e: e.memset(zrow_bf[:, :], 0.0))
    em.v(lambda e: e.tensor_copy(bvrow[:, :], rowstg[:, :]))
    em.wait("sync", "v", em.now("v"))
    bo_ld = em.dma("sync", rowstg[:, :], d["bo_row"].ap(), "ld")
    em.wait("vector", "ld", bo_ld)
    em.v(lambda e: e.tensor_copy(borow[:, :], rowstg[:, :]))
    em.v(lambda e: e.memset(absacc[:, :], 0.0))
    em.v(lambda e: e.memset(absacc2[:, :], 0.0))
    em.v(lambda e: e.memset(magic_b[:, :], MAGIC))
    em.v(lambda e: e.memset(neg8_b[:, :], -8.0))
    init_v = em.now("v")

    def transpose8(src_bf, psum, n=8):
        for k in range(n):
            em.pe(lambda e, k=k: e.transpose(
                psum[:, k * 128:(k + 1) * 128],
                src_bf[:, k * 128:(k + 1) * 128], ident[:, :]))

    def ln_stats(c, mm_, ff, nnb, absmax=None, eps=1e-5):
        xc = x_sb[:, c * DM:(c + 1) * DM]
        em.wait("scalar", "ld", x_ld[c])
        em.sc(lambda e, xc=xc, c=c: e.activation(
            sA[:, :], xc, AF.Square, accum_out=ss_t[:, c:c + 1]))
        em.wait("vector", "ld", x_ld[c])
        em.v(lambda e, xc=xc, c=c: e.tensor_reduce(
            sm_t[:, c:c + 1], xc, AX, OP.add))
        sc_sq = em.now("sc")
        em.wait("vector", "sc", sc_sq)
        em.v(lambda e, c=c: e.tensor_scalar(
            mm_[:, c:c + 1], sm_t[:, c:c + 1], 1.0 / DM, None, OP.mult))
        em.v(lambda e, c=c: e.tensor_scalar(
            sB[:, 0:1], ss_t[:, c:c + 1], 1.0 / DM, eps, OP.mult, OP.add))
        em.v(lambda e, c=c: e.tensor_tensor(
            sB[:, 1:2], mm_[:, c:c + 1], mm_[:, c:c + 1], OP.mult))
        em.v(lambda e: e.tensor_sub(sB[:, 0:1], sB[:, 0:1], sB[:, 1:2]))
        em.v(lambda e: e.reciprocal(sB[:, 2:3], sB[:, 0:1]))
        vv = em.now("v")
        em.wait("scalar", "v", vv)
        em.sc(lambda e, c=c: e.activation(ff[:, c:c + 1], sB[:, 2:3], AF.Sqrt))
        sc_f = em.now("sc")
        em.wait("vector", "sc", sc_f)
        em.v(lambda e, c=c: e.tensor_tensor(
            sB[:, 3:4], mm_[:, c:c + 1], ff[:, c:c + 1], OP.mult))
        em.v(lambda e, c=c: e.tensor_scalar(
            nnb[:, c:c + 1], sB[:, 3:4], -1.0, None, OP.mult))
        if absmax is not None:
            em.v(lambda e, xc=xc, c=c: e.tensor_scalar(
                sB[:, :], xc, mm_[:, c:c + 1], ff[:, c:c + 1],
                OP.subtract, OP.mult))
            em.v(lambda e: e.tensor_reduce(mxc[:, :], sB[:, :], AX, OP.max,
                                           apply_absolute_value=True))
            em.v(lambda e, am=absmax: e.tensor_tensor(
                am[:, :], am[:, :], mxc[:, :], OP.max))

    def allreduce_max(acc_sb, cin, cout, col):
        vv = em.now("v")
        em.wait("gpsimd", "v", vv)
        em.dma("gpsimd", cin.ap()[0, :], acc_sb[:, 0], "gw")
        em.wait("gpsimd", "gw", em.now("gw"))
        em.op("gpsimd", lambda e: e.collective_compute(
            "AllReduce", OP.max, replica_groups=[list(range(NCORES))],
            ins=[cin.ap().opt()], outs=[cout.ap().opt()]), "cc", 1)
        em.wait("gpsimd", "cc", em.now("cc"))
        em.dma("gpsimd", mrow[:, :], cout.ap(), "gw")
        em.wait("vector", "gw", em.now("gw"))
        em.v(lambda e, col=col: e.tensor_reduce(
            smt[:, col:col + 1], mrow[:, :], AX, OP.max))

    def bcast_scalar(src_col, dst_sb):
        vv = em.now("v")
        em.wait("tensor", "v", vv)
        em.pe(lambda e: e.matmul(P[5][:, 0:1], ones_row_f[:, :],
                                 smt[0:1, src_col:src_col + 1],
                                 start=True, stop=True))
        pen = em.now("pe")
        em.wait("vector", "pe", pen)
        em.v(lambda e, dst_sb=dst_sb: e.tensor_copy(dst_sb[:, :], P[5][:, 0:1]))

    em.wait("vector", "v", init_v)
    for c in range(NCH):
        ln_stats(c, m0, f0, nb0, absmax=absacc)

    allreduce_max(absacc, d["cc1_in"], d["cc1_out"], 0)
    em.wait("scalar", "ld", ld_consts)
    em.wait("scalar", "v", em.now("v"))
    em.sc(lambda e: e.activation(smt[:, 4:5], clipx_sb[:, :], AF.Sigmoid))
    em.wait("vector", "sc", em.now("sc"))
    em.v(lambda e: e.tensor_tensor(smt[:, 5:6], smt[:, 4:5], smt[:, 0:1],
                                   OP.mult))
    em.v(lambda e: e.tensor_scalar(smt[:, 6:7], smt[:, 5:6], 1.0 / 127.0,
                                   None, OP.mult))
    em.v(lambda e: e.reciprocal(smt[:, 7:8], smt[:, 6:7]))
    bcast_scalar(7, rsx_b)
    bcast_scalar(6, sx_b)
    vscale1 = em.now("v")

    em.wait("vector", "v", vscale1)
    for c in range(NCH):
        xc = x_sb[:, c * DM:(c + 1) * DM]
        em.v(lambda e, c=c: e.tensor_tensor(qsc[:, :], f0[:, c:c + 1],
                                            rsx_b[:, :], OP.mult))
        em.v(lambda e, c=c: e.tensor_tensor(sB[:, 0:1], m0[:, c:c + 1],
                                            qsc[:, :], OP.mult))
        em.v(lambda e: e.tensor_scalar(qbi[:, :], sB[:, 0:1], -1.0, MAGIC,
                                       OP.mult, OP.add))
        vv = em.now("v")
        em.wait("scalar", "v", vv)
        em.sc(lambda e, xc=xc: e.activation(sA[:, :], xc, AF.Identity,
                                            bias=qbi[:, :], scale=qsc[:, :]))
        em.wait("vector", "sc", em.now("sc"))
        em.v(lambda e: e.tensor_scalar(sB[:, :], sA[:, :], -MAGIC, 127.0,
                                       OP.add, OP.min))
        em.v(lambda e: e.tensor_scalar(tbf[:, :], sB[:, :], -128.0, None,
                                       OP.max))
        em.wait("tensor", "v", em.now("v"))
        em.wait("tensor", "sd", max(0, em.now("sd")))
        transpose8(tbf, T6)
        em.wait("vector", "pe", em.now("pe"))
        em.v(lambda e: e.tensor_copy(tbf2[:, :], T6[:, :]))
        em.wait("scalar", "v", em.now("v"))
        dst = ap3(d["xq_hbm"], c * 128, [[S, 128], [128 * S, 8], [1, 128]])
        em.dma("scalar", dst, ap3(tbf2, 0, [[DM, 128], [128, 8], [1, 128]]),
               "sd")
    b1_v = em.now("v")
    b1_sd = em.now("sd")

    AR = Arena(nc, PHASE_BASE - SBUF_BASE)
    wq_bf = AR.take("wq_bf", [128, KT * DM], BF16)
    wk_bf = AR.take("wk_bf", [128, KT * DM], BF16)
    wv_bf = AR.take("wv_bf", [128, KT * DM], BF16)
    wo_bf = AR.take("wo_bf", [128, KT * DM], BF16)
    xnT_g = AR.take("xnT_g", [128, KT * 512], BF16)
    qT_g = AR.take("qT_g", [128, KT * 512], BF16)
    kT_g = AR.take("kT_g", [128, KT * 512], BF16)
    v_g = AR.take("v_g", [128, 4 * 1040], BF16)
    kA_c = AR.take("kA_c", [128, DM], BF16)
    STb = [AR.take(f"STb{i}", [128, 512], BF16) for i in range(2)]
    state_sb = AR.take("state_sb", [128, 8 * 65], BF16)
    state32 = AR.take("state32", [128, 8 * 65], F32)
    yT_c = AR.take("yT_c", [128, DM], BF16)

    for wname, wsb in (("wq_t", wq_bf), ("wk_t", wk_bf), ("wv_t", wv_bf),
                       ("wo_t", wo_bf)):
        src = ap3(d[wname], 0, [[DM, 128], [128 * DM, KT], [1, DM]])
        dst = ap3(wsb, 0, [[KT * DM, 128], [DM, KT], [1, DM]])
        em.dma("gpsimd", dst, src, "gw")
    l0w_ld = em.now("gw")
    em.v(lambda e: e.memset(
        ap3(v_g, 64, [[4 * 1040, 128], [1040, 4], [65, 16]]), 1.0))
    vg_ones = em.now("v")

    for g in range(NG):
        for cl in range(4):
            c = 4 * g + cl
            xc = x_sb[:, c * DM:(c + 1) * DM]
            em.wait("scalar", "v", b1_v)
            em.wait("scalar", "pe", em.now("pe"))
            em.sc(lambda e, xc=xc, c=c: e.activation(
                tbf[:, :], xc, AF.Identity, bias=nb0[:, c:c + 1],
                scale=f0[:, c:c + 1]))
            em.wait("tensor", "sc", em.now("sc"))
            em.wait("tensor", "v", em.now("v"))
            transpose8(tbf, T6)
            em.wait("vector", "pe", em.now("pe"))
            em.v(lambda e, cl=cl: e.tensor_copy(
                ap3(xnT_g, 128 * cl, [[KT * 512, 128], [512, 8], [1, 128]]),
                ap3(T6, 0, [[1024, 128], [128, 8], [1, 128]])))
        em.wait("tensor", "v", em.now("v"))
        em.wait("tensor", "gw", l0w_ld)

        b2_cons_sc = {0: em.now("sc"), 1: em.now("sc")}
        for wsb, dstT, bias in ((wq_bf, qT_g, bq_sb), (wk_bf, kT_g, bk_sb)):
            for ot in range(8):
                pb = P[ot % 2]
                em.wait("tensor", "sc", b2_cons_sc[ot % 2])
                for kt in range(KT):
                    em.pe(lambda e, pb=pb, wsb=wsb, ot=ot, kt=kt: e.matmul(
                        pb[:, :],
                        wsb[:, DM * kt + 128 * ot: DM * kt + 128 * (ot + 1)],
                        xnT_g[:, 512 * kt:512 * (kt + 1)],
                        start=(kt == 0), stop=(kt == KT - 1)))
                pen = em.now("pe")
                em.wait("scalar", "pe", pen)
                em.sc(lambda e, pb=pb, dstT=dstT, bias=bias, ot=ot:
                      e.activation(dstT[:, 512 * ot:512 * (ot + 1)], pb[:, :],
                                   AF.Exp, bias=bias[:, ot:ot + 1]))
                b2_cons_sc[ot % 2] = em.now("sc")
        em.wait("tensor", "v", vg_ones)
        va_cons = {0: em.now("v"), 1: em.now("v")}
        for cl in range(4):
            for oh in range(2):
                pb = P[2 + (2 * cl + oh) % 2]
                em.wait("tensor", "v", va_cons[(2 * cl + oh) % 2])
                for kt in range(KT):
                    em.pe(lambda e, pb=pb, cl=cl, oh=oh, kt=kt: e.matmul(
                        pb[:, :],
                        xnT_g[:, 512 * kt + 128 * cl: 512 * kt + 128 * (cl + 1)],
                        wv_bf[:, DM * kt + 512 * oh: DM * kt + 512 * (oh + 1)],
                        start=(kt == 0), stop=False))
                em.pe(lambda e, pb=pb, oh=oh: e.matmul(
                    pb[:, :], ones_row_bf[:, :],
                    bvrow[:, 512 * oh:512 * (oh + 1)], start=False, stop=True))
                pen = em.now("pe")
                em.wait("vector", "pe", pen)
                em.v(lambda e, pb=pb, cl=cl, oh=oh: e.tensor_copy(
                    ap3(v_g, 1040 * cl + 65 * 8 * oh,
                        [[4 * 1040, 128], [65, 8], [1, 64]]),
                    ap3(pb, 0, [[512, 128], [64, 8], [1, 64]])))
                va_cons[(2 * cl + oh) % 2] = em.now("v")
        qkv_v = em.now("v")
        qkv_sc = em.now("sc")

        for cl in range(4):
            c = 4 * g + cl
            em.wait("tensor", "sc", qkv_sc)
            em.wait("tensor", "v", em.now("v"))
            for ot in range(8):
                em.pe(lambda e, ot=ot, cl=cl: e.transpose(
                    T7[:, 128 * ot:128 * (ot + 1)],
                    kT_g[:, 512 * ot + 128 * cl: 512 * ot + 128 * (cl + 1)],
                    ident[:, :]))
            em.wait("vector", "pe", em.now("pe"))
            em.wait("vector", "v", qkv_v)
            em.v(lambda e: e.tensor_copy(kA_c[:, :], T7[:, :]))
            ka_v = em.now("v")
            em.wait("tensor", "v", ka_v)

            for hb in range(4):
                ot_bank = P[2 + hb % 2]
                em.wait("tensor", "v", em.now("v"))
                for hh in range(4):
                    h = 4 * hb + hh
                    po, pof = h // 2, 64 * (h % 2)
                    em.pe(lambda e, hh=hh, po=po, pof=pof, cl=cl: e.matmul(
                        P[0][:, 128 * hh:128 * (hh + 1)],
                        kT_g[pof:pof + 64,
                             512 * po + 128 * cl: 512 * po + 128 * (cl + 1)],
                        qT_g[pof:pof + 64,
                             512 * po + 128 * cl: 512 * po + 128 * (cl + 1)],
                        start=True, stop=True))
                em.wait("vector", "pe", em.now("pe"))
                em.v(lambda e, hb=hb: e.tensor_tensor(
                    STb[hb % 2][:, :], P[0][:, :], causal4[:, :], OP.mult))
                em.wait("tensor", "v", em.now("v"))
                for hh in range(4):
                    h = 4 * hb + hh
                    po, pof = h // 2, 64 * (h % 2)
                    em.pe(lambda e, h=h, hh=hh, hb=hb, cl=cl, c=c,
                          ot_bank=ot_bank: e.matmul(
                        ot_bank[0:65, 128 * hh:128 * (hh + 1)],
                        v_g[:, 1040 * cl + 65 * h: 1040 * cl + 65 * h + 65],
                        STb[hb % 2][:, 128 * hh:128 * (hh + 1)],
                        start=True, stop=(c == 0)))
                    if c > 0:
                        em.pe(lambda e, hh=hh, cl=cl, po=po, pof=pof,
                              ot_bank=ot_bank: e.matmul(
                            ot_bank[0:65, 128 * hh:128 * (hh + 1)],
                            state_sb[pof:pof + 64, 65 * po:65 * po + 65],
                            qT_g[pof:pof + 64,
                                 512 * po + 128 * cl: 512 * po + 128 * (cl + 1)],
                            start=False, stop=True))
                if c < NCH - 1:
                    for hh in range(4):
                        h = 4 * hb + hh
                        po, pof = h // 2, 64 * (h % 2)
                        em.pe(lambda e, h=h, cl=cl, po=po, pof=pof: e.matmul(
                            P[4][pof:pof + 64, 64 * po:64 * (po + 1)],
                            kA_c[:, 64 * h:64 * (h + 1)],
                            v_g[:, 1040 * cl + 65 * h: 1040 * cl + 65 * h + 64],
                            start=True, stop=True))
                        em.pe(lambda e, h=h, po=po, pof=pof, cl=cl: e.matmul(
                            P[5][pof:pof + 64, po:po + 1],
                            kA_c[:, 64 * h:64 * (h + 1)], onescol[:, :],
                            start=True, stop=True))
                pen = em.now("pe")
                em.wait("vector", "pe", pen)
                em.v(lambda e, ot_bank=ot_bank: e.reciprocal(
                    recd[:, :], ot_bank[64:65, :]))
                em.wait("tensor", "v", em.now("v"))
                em.pe(lambda e: e.matmul(P[1][0:64, :], ones_row_f[0:1, 0:64],
                                         recd[:, :], start=True, stop=True))
                em.wait("vector", "pe", em.now("pe"))
                em.v(lambda e: e.tensor_copy(recb[0:64, :], P[1][0:64, :]))
                for hh in range(4):
                    h = 4 * hb + hh
                    if h % 2 == 0:
                        dstap = yT_c[0:64, 128 * (h // 2):128 * (h // 2) + 128]
                    else:
                        dstap = ystg[0:64, 128 * (h // 2):128 * (h // 2) + 128]
                    em.v(lambda e, hh=hh, dstap=dstap, ot_bank=ot_bank:
                         e.tensor_tensor(
                        dstap, ot_bank[0:64, 128 * hh:128 * (hh + 1)],
                        recb[0:64, 128 * hh:128 * (hh + 1)], OP.mult))
            if c < NCH - 1:
                em.wait("vector", "pe", em.now("pe"))
                if c == 0:
                    em.v(lambda e: e.tensor_copy(
                        ap3(state32, 0, [[520, 128], [65, 8], [1, 64]]),
                        ap3(P[4], 0, [[512, 128], [64, 8], [1, 64]])))
                    em.v(lambda e: e.tensor_copy(
                        ap3(state32, 64, [[520, 128], [65, 8]]),
                        ap3(P[5], 0, [[512, 128], [1, 8]])))
                else:
                    em.v(lambda e: e.tensor_tensor(
                        ap3(state32, 0, [[520, 128], [65, 8], [1, 64]]),
                        ap3(state32, 0, [[520, 128], [65, 8], [1, 64]]),
                        ap3(P[4], 0, [[512, 128], [64, 8], [1, 64]]),
                        OP.add))
                    em.v(lambda e: e.tensor_tensor(
                        ap3(state32, 64, [[520, 128], [65, 8]]),
                        ap3(state32, 64, [[520, 128], [65, 8]]),
                        ap3(P[5], 0, [[512, 128], [1, 8]]),
                        OP.add))
                em.v(lambda e: e.tensor_copy(state_sb[:, :], state32[:, :]))
            em.wait("sync", "v", em.now("v"))
            em.dma("sync", yT_c[64:128, :], ystg[0:64, :], "ld")
            em.wait("tensor", "ld", em.now("ld"))
            for oh in range(2):
                pb = P[0] if oh == 0 else P[1]
                for kt in range(KT):
                    em.pe(lambda e, pb=pb, kt=kt, oh=oh: e.matmul(
                        pb[:, :], yT_c[:, 128 * kt:128 * (kt + 1)],
                        wo_bf[:, DM * kt + 512 * oh: DM * kt + 512 * (oh + 1)],
                        start=(kt == 0), stop=False))
                em.pe(lambda e, pb=pb, oh=oh: e.matmul(
                    pb[:, :], ones_row_bf[:, :],
                    borow[:, 512 * oh:512 * (oh + 1)], start=False, stop=True))
                pen = em.now("pe")
                em.wait("vector", "pe", pen)
                xslice = x_sb[:, c * DM + 512 * oh: c * DM + 512 * (oh + 1)]
                em.v(lambda e, xs=xslice, pb=pb: e.tensor_add(xs, xs, pb[:, :]))
    attn_done_v = em.now("v")
    attn_done_pe = em.now("pe")

    AR = Arena(nc, PHASE_BASE - SBUF_BASE)
    wgT_q = AR.take("wgT_q", [128, KT * I0], BF16)
    wuT_q = AR.take("wuT_q", [128, KT * I0], BF16)
    wstg = [AR.take(f"wstg{i}", [128, DM], F32) for i in range(2)]
    xqtb = AR.take("xqtb", [128, KT * 512], BF16)
    h16t = [AR.take(f"h16t{i}", [128, 512], FP16) for i in range(2)]

    em.wait("sync", "pe", attn_done_pe)
    em.wait("sync", "v", attn_done_v)
    stg_consumed = [0, 0]
    for wi, (wname, wdst) in enumerate((("wg", wgT_q), ("wu", wuT_q))):
        for it in range(NI0):
            j = wi * NI0 + it
            stg = wstg[j % 2]
            em.wait("sync", "sc", stg_consumed[j % 2])
            ldt = em.dma("sync", stg[:, :],
                         d[wname].ap()[128 * it:128 * (it + 1), :], "ld")
            em.wait("vector", "ld", ldt)
            em.v(lambda e, stg=stg: e.tensor_reduce(
                mxc[:, :], stg[:, :], AX, OP.max, apply_absolute_value=True))
            em.v(lambda e: e.reciprocal(qsc[:, :], mxc[:, :]))
            em.v(lambda e: e.tensor_scalar(qsc[:, :], qsc[:, :], 127.0, None,
                                           OP.mult))
            em.v(lambda e: e.tensor_scalar(qbi[:, :], mxc[:, :], 1.0 / 127.0,
                                           None, OP.mult))
            vv = em.now("v")
            em.wait("scalar", "v", vv)
            em.sc(lambda e, stg=stg: e.activation(
                sA[:, :], stg[:, :], AF.Identity, bias=magic_b[:, :],
                scale=qsc[:, :]))
            stg_consumed[j % 2] = em.now("sc")
            em.wait("vector", "sc", em.now("sc"))
            em.v(lambda e: e.tensor_scalar(tbf[:, :], sA[:, :], -MAGIC,
                                           qbi[:, :], OP.add, OP.mult))
            em.wait("tensor", "v", em.now("v"))
            transpose8(tbf, T6)
            em.wait("vector", "pe", em.now("pe"))
            em.v(lambda e, wdst=wdst, it=it: e.tensor_copy(
                ap3(wdst, 128 * it, [[KT * I0, 128], [I0, 8], [1, 128]]),
                ap3(T6, 0, [[1024, 128], [128, 8], [1, 128]])))
    wgq_v = em.now("v")

    em.wait("sync", "sd", b1_sd)
    gu_cons_sc = {0: em.now("sc"), 1: em.now("sc")}
    gu_cons_v = {0: em.now("v"), 1: em.now("v")}
    for g in range(NG):
        src = ap3(d["xq_hbm"], 512 * g, [[S, 128], [128 * S, KT], [1, 512]])
        dstx = ap3(xqtb, 0, [[KT * 512, 128], [512, KT], [1, 512]])
        em.wait("sync", "pe", em.now("pe"))
        xq_ld = em.dma("sync", dstx, src, "ld")
        em.wait("tensor", "ld", xq_ld)
        em.wait("tensor", "v", wgq_v)
        for it in range(NI0):
            pg = P[it % 2]
            pu = P[2 + it % 2]
            em.wait("tensor", "sc", gu_cons_sc[it % 2])
            em.wait("tensor", "v", gu_cons_v[it % 2])
            for kt in range(KT):
                em.pe(lambda e, pg=pg, it=it, kt=kt: e.matmul(
                    pg[:, :],
                    wgT_q[:, I0 * kt + 128 * it: I0 * kt + 128 * (it + 1)],
                    xqtb[:, 512 * kt:512 * (kt + 1)],
                    start=(kt == 0), stop=(kt == KT - 1)))
            for kt in range(KT):
                em.pe(lambda e, pu=pu, it=it, kt=kt: e.matmul(
                    pu[:, :],
                    wuT_q[:, I0 * kt + 128 * it: I0 * kt + 128 * (it + 1)],
                    xqtb[:, 512 * kt:512 * (kt + 1)],
                    start=(kt == 0), stop=(kt == KT - 1)))
            pen = em.now("pe")
            em.wait("scalar", "pe", pen)
            em.sc(lambda e, pg=pg: e.activation(
                sA[:, 0:512], pg[:, :], AF.Silu, scale=sx_b[:, :]))
            gu_cons_sc[it % 2] = em.now("sc")
            em.wait("vector", "sc", em.now("sc"))
            hb_ = h16t[it % 2]
            em.wait("vector", "sd", max(0, em.now("sd") - 16))
            em.v(lambda e, pu=pu, hb_=hb_: e.tensor_tensor(
                hb_[:, :], sA[:, 0:512], pu[:, :], OP.mult))
            gu_cons_v[it % 2] = em.now("v")
            em.v(lambda e, hb_=hb_: e.tensor_reduce(
                mxc[:, :], hb_[:, :], AX, OP.max, apply_absolute_value=True))
            em.v(lambda e: e.tensor_tensor(absacc2[:, :], absacc2[:, :],
                                           mxc[:, :], OP.max))
            em.wait("scalar", "v", em.now("v"))
            em.dma("scalar",
                   d["h_hbm"].ap()[128 * it:128 * (it + 1),
                                   512 * g:512 * (g + 1)],
                   hb_[:, :], "sd")
    gu_pe = em.now("pe")
    gu_sd = em.now("sd")

    allreduce_max(absacc2, d["cc2_in"], d["cc2_out"], 1)
    em.wait("scalar", "v", em.now("v"))
    em.sc(lambda e: e.activation(smt[:, 8:9], clipd_sb[:, :], AF.Sigmoid))
    em.wait("vector", "sc", em.now("sc"))
    em.v(lambda e: e.tensor_tensor(smt[:, 9:10], smt[:, 8:9], smt[:, 1:2],
                                   OP.mult))
    em.v(lambda e: e.reciprocal(smt[:, 10:11], smt[:, 9:10]))
    em.v(lambda e: e.tensor_scalar(smt[:, 11:12], smt[:, 10:11], 127.0, None,
                                   OP.mult))
    em.v(lambda e: e.tensor_tensor(smt[:, 12:13], smt[:, 9:10], smt[:, 6:7],
                                   OP.mult))
    em.v(lambda e: e.tensor_scalar(smt[:, 13:14], smt[:, 12:13], 1.0 / 127.0,
                                   None, OP.mult))
    bcast_scalar(11, rh_b)
    bcast_scalar(13, sh_b)
    ffn2_scales_v = em.now("v")

    AR = Arena(nc, PHASE_BASE - SBUF_BASE)
    wdT_q = AR.take("wdT_q", [128, NI0 * DM], BF16)
    hq_bf = AR.take("hq_bf", [128, I0], BF16)
    hq2 = AR.take("hq2", [128, I0], BF16)
    hq_pair = [hq_bf, hq2]
    tD = AR.take("tD", [128, I0], F32)
    ov_base = AR.off - SBUF_BASE
    AR2 = Arena(nc, ov_base)
    wdstg = [AR2.take(f"wdstg{i}", [128, I0], F32) for i in range(2)]
    AR3 = Arena(nc, ov_base)
    hbuf = [AR3.take(f"hbuf{i}", [128, NI0 * 512], FP16) for i in range(2)]

    em.wait("sync", "pe", gu_pe)
    em.wait("sync", "v", em.now("v"))
    wd_consumed = [0, 0]
    for dt in range(8):
        stg = wdstg[dt % 2]
        em.wait("sync", "sc", wd_consumed[dt % 2])
        wd_ldt = em.dma("sync", stg[:, :],
                        d["wd"].ap()[128 * dt:128 * (dt + 1), :], "ld")
        em.wait("vector", "ld", wd_ldt)
        em.v(lambda e, stg=stg: e.tensor_reduce(
            mxc[:, :], stg[:, :], AX, OP.max, apply_absolute_value=True))
        em.v(lambda e: e.reciprocal(qsc[:, :], mxc[:, :]))
        em.v(lambda e: e.tensor_scalar(qsc[:, :], qsc[:, :], 127.0, None,
                                       OP.mult))
        em.v(lambda e: e.tensor_scalar(qbi[:, :], mxc[:, :], 1.0 / 127.0,
                                       None, OP.mult))
        em.wait("scalar", "v", em.now("v"))
        em.sc(lambda e, stg=stg: e.activation(
            tD[:, :], stg[:, :], AF.Identity, bias=magic_b[:, :],
            scale=qsc[:, :]))
        wd_consumed[dt % 2] = em.now("sc")
        em.wait("vector", "sc", em.now("sc"))
        em.v(lambda e: e.tensor_scalar(hq_bf[:, :], tD[:, :], -MAGIC,
                                       qbi[:, :], OP.add, OP.mult))
        for b0 in range(0, NI0, 8):
            nn = min(8, NI0 - b0)
            em.wait("tensor", "v", em.now("v"))
            for k in range(nn):
                em.pe(lambda e, b0=b0, k=k: e.transpose(
                    T6[:, 128 * k:128 * (k + 1)],
                    hq_bf[:, 128 * (b0 + k):128 * (b0 + k + 1)], ident[:, :]))
            em.wait("vector", "pe", em.now("pe"))
            em.v(lambda e, b0=b0, nn=nn, dt=dt: e.tensor_copy(
                ap3(wdT_q, DM * b0 + 128 * dt,
                    [[NI0 * DM, 128], [DM, nn], [1, 128]]),
                ap3(T6, 0, [[1024, 128], [128, nn], [1, 128]])))
    wd_v = em.now("v")

    em.wait("sync", "sd", gu_sd)
    em.wait("scalar", "v", wd_v)
    em.wait("sync", "sc", em.now("sc"))
    em.wait("sync", "v", em.now("v"))
    h_consumed = [0, 0]
    h_ldt_g = [0]
    hq_ready = {0: 0, 1: 0}

    def dp_quant(c):
        grp, cl_ = c // 4, c % 4
        hb_ = hbuf[grp % 2]
        if cl_ == 0:
            src = ap3(d["h_hbm"], 512 * grp,
                      [[S, 128], [128 * S, NI0], [1, 512]])
            dsth = ap3(hb_, 0, [[NI0 * 512, 128], [512, NI0], [1, 512]])
            em.wait("sync", "sc", h_consumed[grp % 2])
            h_ldt_g[0] = em.dma("sync", dsth, src, "ld")
        em.wait("scalar", "ld", h_ldt_g[0])
        em.wait("scalar", "v", ffn2_scales_v)
        em.wait("scalar", "v", em.now("v"))
        em.sc(lambda e, hb_=hb_, cl_=cl_: e.activation(
            ap3(tD, 0, [[I0, 128], [128, NI0], [1, 128]]),
            ap3(hb_, 128 * cl_, [[NI0 * 512, 128], [512, NI0], [1, 128]]),
            AF.Identity, bias=magic_b[:, :], scale=rh_b[:, :]))
        if cl_ == 3:
            h_consumed[grp % 2] = em.now("sc")
        em.wait("vector", "sc", em.now("sc"))
        em.v(lambda e: e.tensor_scalar(tD[:, :], tD[:, :], -MAGIC, 127.0,
                                       OP.add, OP.min))
        em.v(lambda e, hq=hq_pair[c % 2]: e.tensor_scalar(
            hq[:, :], tD[:, :], -128.0, None, OP.max))
        hq_ready[c % 2] = em.now("v")

    dp_quant(0)
    for c in range(NCH):
        if c + 1 < NCH:
            dp_quant(c + 1)
        em.wait("tensor", "v", hq_ready[c % 2])
        em.wait("tensor", "v", wd_v)
        for oh in range(2):
            pb = P[oh]
            for it in range(NI0):
                em.pe(lambda e, pb=pb, it=it, oh=oh, hq=hq_pair[c % 2]:
                      e.matmul(
                    pb[:, :], hq[:, 128 * it:128 * (it + 1)],
                    wdT_q[:, DM * it + 512 * oh: DM * it + 512 * (oh + 1)],
                    start=(it == 0), stop=(it == NI0 - 1)))
            pen = em.now("pe")
            em.wait("vector", "pe", pen)
            xs = x_sb[:, c * DM + 512 * oh: c * DM + 512 * (oh + 1)]
            em.v(lambda e, xs=xs, pb=pb: e.scalar_tensor_tensor(
                xs, pb[:, :], sh_b[:, :], xs, OP.mult, OP.add))
    l0_done_v = em.now("v")

    if dbg == "l0" or 1 not in layers:
        for c in range(NCH):
            em.wait("scalar", "v", l0_done_v)
            em.dma("scalar", d["dbg0" if dbg else "out_x"].ap()
                   [128 * c:128 * (c + 1), :],
                   x_sb[:, c * DM:(c + 1) * DM], "sd")
        em.wait("scalar", "sd", em.now("sd"))
        em.play()
        ctx.close()
        return nc, d

    HD = 64
    rope_sb = nc.alloc_sbuf_tensor_at("rope_sb", [128, 64], F32,
                                      offset=SBUF_BASE + 212566 + 10)

    for c in range(NCH):
        ln_stats(c, m1, f1, nb1)
    em.wait("scalar", "v", em.now("v"))
    em.sc(lambda e: e.activation(b1q8[:, :], b1q[:, :], AF.Copy))
    em.sc(lambda e: e.activation(b1q[:, :], b1q8[:, :], AF.Copy))

    AQ = Arena(nc, PHASE_BASE - SBUF_BASE)
    a_sb = AQ.take("a_sb", [128, NCH * DM], F32)
    AEND = (AQ.off + 63) & ~63
    for c in range(NCH):
        xc = x_sb[:, c * DM:(c + 1) * DM]
        ac = a_sb[:, c * DM:(c + 1) * DM]
        em.wait("vector", "pe", em.now("pe"))
        em.v(lambda e, xc=xc, ac=ac, c=c: e.tensor_scalar(
            ac, xc, m1[:, c:c + 1], f1[:, c:c + 1], OP.subtract, OP.mult))
        vv = em.now("v")
        em.wait("scalar", "v", vv)
        em.wait("scalar", "sd", em.now("sd"))
        em.sc(lambda e, ac=ac: e.activation(t8[:, :], ac, AF.Copy))
        em.wait("vector", "sc", em.now("sc"))
        em.v(lambda e: e.tensor_copy(tbf[:, :], t8[:, :]))
        em.wait("tensor", "v", em.now("v"))
        em.wait("tensor", "sd", em.now("sd"))
        transpose8(tbf, T6)
        em.wait("vector", "pe", em.now("pe"))
        em.v(lambda e: e.tensor_copy(t8[:, :], T6[:, :]))
        em.wait("scalar", "v", em.now("v"))
        dst = ap3(d["xq8_hbm"], c * 128, [[S, 128], [128 * S, 8], [1, 128]])
        em.dma("scalar", dst, ap3(t8, 0, [[DM, 128], [128, 8], [1, 128]]),
               "sd")
        em.dma("scalar", d["x_hbm"].ap()[128 * c:128 * (c + 1), :], xc, "sd")
    xsp_sd = em.now("sd")

    AX_ = Arena(nc, 0, limit=65664)
    kT1 = AX_.take("kT1", [128, 2 * S], BF16)
    v1_sb = AX_.take("v1_sb", [128, 16 * 260], BF16)
    qT1 = AX_.take("qT1", [128, KT * 512], BF16)
    xrT = AX_.take("xrT", [128, KT * S], BF16)
    yT1 = AX_.take("yT1", [128, KT * 512], BF16)
    AU = Arena(nc, AEND - SBUF_BASE)
    gwq_bf = AU.take("gwq_bf", [128, KT * DM], BF16)
    gkvd_bf = AU.take("gkvd_bf", [128, KT * 256], BF16)
    gkup_bf = AU.take("gkup_bf", [128, 2 * 256], BF16)
    gvup_bf = AU.take("gvup_bf", [128, 2 * 256], BF16)
    gwo_bf = AU.take("gwo_bf", [128, KT * DM], BF16)
    latT_c = AU.take("latT_c", [128, 256], BF16)
    PT = [AU.take(f"PT{i}", [128, 512], BF16) for i in range(2)]
    ystg1 = AU.take("ystg1", [64, 512], BF16)

    l0_end_pe, l0_end_v = em.now("pe"), em.now("v")
    em.wait("gpsimd", "pe", l0_end_pe)
    em.wait("gpsimd", "v", l0_end_v)
    for wname, wsb, wid in (("gwq", gwq_bf, DM), ("gkvd", gkvd_bf, 256),
                            ("gkup", gkup_bf, 256), ("gvup", gvup_bf, 256),
                            ("gwo", gwo_bf, DM)):
        nk = 8 if wid == DM or wname == "gkvd" else 2
        nk = {"gwq": 8, "gkvd": 8, "gkup": 2, "gvup": 2, "gwo": 8}[wname]
        src = ap3(d[wname], 0, [[wid, 128], [128 * wid, nk], [1, wid]])
        dstw = ap3(wsb, 0, [[nk * wid, 128], [wid, nk], [1, wid]])
        em.dma("gpsimd", dstw, src, "gw")
    gl_w = em.now("gw")
    em.wait("vector", "gw", gl_w)
    for kt in range(KT):
        em.v(lambda e, kt=kt: e.tensor_scalar(
            gwq_bf[:, DM * kt:DM * (kt + 1)],
            gwq_bf[:, DM * kt:DM * (kt + 1)],
            n1w_sb[:, kt:kt + 1], None, OP.mult))
        em.v(lambda e, kt=kt: e.tensor_scalar(
            gkvd_bf[:, 256 * kt:256 * (kt + 1)],
            gkvd_bf[:, 256 * kt:256 * (kt + 1)],
            n1w_sb[:, kt:kt + 1], None, OP.mult))
    gw_fold_v = em.now("v")

    def rope_norm(width, nheads, fcol):
        em.wait("scalar", "v", em.now("v"))
        em.sc(lambda e: e.activation(sB[:, 0:width], sA[:, 0:width],
                                     AF.Square))
        em.wait("vector", "sc", em.now("sc"))
        em.v(lambda e: e.tensor_reduce(
            hdf[:, fcol:fcol + nheads],
            ap3(sB, 0, [[DM, 128], [64, nheads], [1, 64]]), AX, OP.add))
        em.v(lambda e: e.tensor_scalar(
            hdf[:, fcol:fcol + nheads], hdf[:, fcol:fcol + nheads],
            1.0 / HD, 1e-6, OP.mult, OP.add))
        em.v(lambda e: e.reciprocal(hdf[:, fcol:fcol + nheads],
                                    hdf[:, fcol:fcol + nheads]))
        em.wait("scalar", "v", em.now("v"))
        em.sc(lambda e: e.activation(hdf[:, fcol:fcol + nheads],
                                     hdf[:, fcol:fcol + nheads], AF.Sqrt))
        em.wait("vector", "sc", em.now("sc"))
        ev = ap3(sA, 0, [[DM, 128], [64, nheads], [2, 32]])
        od = ap3(sA, 1, [[DM, 128], [64, nheads], [2, 32]])
        rev = ap3(sB, 0, [[DM, 128], [64, nheads], [2, 32]])
        rod = ap3(sB, 1, [[DM, 128], [64, nheads], [2, 32]])
        cosb = ap3(rope_sb, 0, [[64, 128], [0, nheads], [1, 32]])
        sinb = ap3(rope_sb, 32, [[64, 128], [0, nheads], [1, 32]])
        em.v(lambda e: e.tensor_tensor(rev, ev, cosb, OP.mult))
        em.v(lambda e: e.tensor_tensor(rod, ev, sinb, OP.mult))
        em.v(lambda e: e.tensor_tensor(ev, od, sinb, OP.mult))
        em.v(lambda e: e.tensor_tensor(rev, rev, ev, OP.subtract))
        em.v(lambda e: e.tensor_tensor(ev, od, cosb, OP.mult))
        em.v(lambda e: e.tensor_tensor(rod, rod, ev, OP.add))
        fb = ap3(hdf, fcol, [[16, 128], [1, nheads], [0, 64]])
        em.v(lambda e: e.tensor_tensor(
            ap3(tbf, 0, [[DM, 128], [64, nheads], [1, 64]]),
            ap3(sB, 0, [[DM, 128], [64, nheads], [1, 64]]), fb, OP.mult))
        return em.now("v")

    def l1_attn(prefix, tables, kvd_mode, wq_sb, wo_sb, wk_sb=None,
                wv_sb=None, fr_sb=None):
        rope_v = 0
        ystg1_dma = [0]
        last_exp_sc = {0: 0, 1: 0}
        bank_free_v = {0: 0, 1: 0}
        sc_base = [0]
        for c in range(NCH):
            ac = a_sb[:, c * DM:(c + 1) * DM]
            em.wait("sync", "v", rope_v)
            em.dma("sync", rope_sb[:, 0:32],
                   d[tables].ap()[:, 32 * c:32 * c + 32], "ld")
            rtl = em.dma("sync", rope_sb[:, 32:64],
                         d[tables].ap()[:, 512 + 32 * c:512 + 32 * c + 32],
                         "ld")
            em.wait("scalar", "v", em.now("v"))
            em.sc(lambda e, ac=ac, c=c: e.activation(
                sA[:, :], ac, AF.Square, accum_out=ss_t[:, c:c + 1]))
            em.wait("vector", "sc", em.now("sc"))
            em.v(lambda e, c=c: e.tensor_scalar(
                sB[:, 0:1], ss_t[:, c:c + 1], 1.0 / DM, 1e-6, OP.mult,
                OP.add))
            em.v(lambda e: e.reciprocal(sB[:, 1:2], sB[:, 0:1]))
            em.wait("scalar", "v", em.now("v"))
            em.sc(lambda e, c=c: e.activation(fr1[:, c:c + 1], sB[:, 1:2],
                                              AF.Sqrt))
            em.wait("vector", "sc", em.now("sc"))
            em.wait("vector", "pe", em.now("pe"))
            em.v(lambda e, ac=ac, c=c: e.tensor_scalar(
                tbf[:, :], ac, fr1[:, c:c + 1], None, OP.mult))
            em.wait("tensor", "v", em.now("v"))
            transpose8(tbf, T6)
            em.wait("vector", "pe", em.now("pe"))
            em.v(lambda e, c=c: e.tensor_copy(
                ap3(xrT, 128 * c, [[KT * S, 128], [S, 8], [1, 128]]),
                ap3(T6, 0, [[1024, 128], [128, 8], [1, 128]])))
            em.wait("tensor", "v", em.now("v"))
            em.wait("tensor", "gw", em.now("gw"))
            if kvd_mode:
                for lt in range(2):
                    for kt in range(KT):
                        em.pe(lambda e, lt=lt, kt=kt, c=c: e.matmul(
                            P[lt][:, 0:128],
                            gkvd_bf[:, 256 * kt + 128 * lt:
                                    256 * kt + 128 * (lt + 1)],
                            xrT[:, S * kt + 128 * c:S * kt + 128 * (c + 1)],
                            start=(kt == 0), stop=(kt == KT - 1)))
                em.wait("vector", "pe", em.now("pe"))
                for lt in range(2):
                    em.v(lambda e, lt=lt: e.tensor_copy(
                        latT_c[:, 128 * lt:128 * (lt + 1)], P[lt][:, 0:128]))
                em.wait("tensor", "v", em.now("v"))
                for lt in range(2):
                    em.pe(lambda e, lt=lt: e.matmul(
                        P[2][:, 0:256], latT_c[:, 128 * lt:128 * (lt + 1)],
                        gkup_bf[:, 256 * lt:256 * (lt + 1)],
                        start=(lt == 0), stop=(lt == 1)))
                for lt in range(2):
                    em.pe(lambda e, lt=lt: e.matmul(
                        P[3][:, 0:256], latT_c[:, 128 * lt:128 * (lt + 1)],
                        gvup_bf[:, 256 * lt:256 * (lt + 1)],
                        start=(lt == 0), stop=(lt == 1)))
            else:
                for kt in range(KT):
                    em.pe(lambda e, kt=kt, c=c: e.matmul(
                        P[2][:, 0:256],
                        xrT[:, S * kt + 128 * c:S * kt + 128 * (c + 1)],
                        wk_sb[:, 256 * kt:256 * (kt + 1)],
                        start=(kt == 0), stop=(kt == KT - 1)))
                for kt in range(KT):
                    em.pe(lambda e, kt=kt, c=c: e.matmul(
                        P[3][:, 0:256],
                        xrT[:, S * kt + 128 * c:S * kt + 128 * (c + 1)],
                        wv_sb[:, 256 * kt:256 * (kt + 1)],
                        start=(kt == 0), stop=(kt == KT - 1)))
            pen = em.now("pe")
            em.wait("scalar", "pe", pen)
            em.sc(lambda e: e.activation(sA[:, 0:256], P[2][:, 0:256],
                                         AF.Copy))
            em.wait("vector", "pe", pen)
            em.v(lambda e, c=c: e.tensor_copy(
                ap3(v1_sb, 260 * c, [[16 * 260, 128], [65, 4], [1, 64]]),
                ap3(P[3], 0, [[512, 128], [64, 4], [1, 64]])))
            em.wait("vector", "ld", rtl)
            em.wait("vector", "sc", em.now("sc"))
            rope_norm(256, 4, 0)
            rope_v = em.now("v")
            em.wait("tensor", "v", rope_v)
            for lt in range(2):
                em.pe(lambda e, lt=lt: e.transpose(
                    T7[:, 128 * lt:128 * (lt + 1)],
                    tbf[:, 128 * lt:128 * (lt + 1)], ident[:, :]))
            em.wait("vector", "pe", em.now("pe"))
            em.v(lambda e, c=c: e.tensor_copy(
                ap3(kT1, 128 * c, [[2 * S, 128], [S, 2], [1, 128]]),
                ap3(T7, 0, [[1024, 128], [128, 2], [1, 128]])))
        prep_v = em.now("v")

        for g in range(NG):
            for cl in range(4):
                c = 4 * g + cl
                em.wait("sync", "v", rope_v)
                em.dma("sync", rope_sb[:, 0:32],
                       d[tables].ap()[:, 32 * c:32 * c + 32], "ld")
                rtl = em.dma("sync", rope_sb[:, 32:64],
                             d[tables].ap()[:, 512 + 32 * c:512 + 32 * c + 32],
                             "ld")
                em.wait("tensor", "sc", em.now("sc"))
                em.wait("tensor", "v", em.now("v"))
                for oh in range(2):
                    for kt in range(KT):
                        em.pe(lambda e, oh=oh, kt=kt, c=c: e.matmul(
                            P[oh][:, :],
                            xrT[:, S * kt + 128 * c:S * kt + 128 * (c + 1)],
                            wq_sb[:, DM * kt + 512 * oh:
                                  DM * kt + 512 * (oh + 1)],
                            start=(kt == 0), stop=(kt == KT - 1)))
                pen = em.now("pe")
                em.wait("scalar", "pe", pen)
                for oh in range(2):
                    em.sc(lambda e, oh=oh: e.activation(
                        sA[:, 512 * oh:512 * (oh + 1)], P[oh][:, :], AF.Copy))
                em.wait("vector", "ld", rtl)
                em.wait("vector", "sc", em.now("sc"))
                rope_norm(DM, 16, 0)
                rope_v = em.now("v")
                em.wait("tensor", "v", rope_v)
                transpose8(tbf, T6)
                em.wait("vector", "pe", em.now("pe"))
                em.v(lambda e, cl=cl: e.tensor_copy(
                    ap3(qT1, 128 * cl, [[KT * 512, 128], [512, 8], [1, 128]]),
                    ap3(T6, 0, [[1024, 128], [128, 8], [1, 128]])))
            qprep_v = em.now("v")
            sc_base[0] = em.now("sc")

            if prefix == "g":
                jbs = list(range(4 * g + 4))
            else:
                jbs = list(range(max(0, 4 * g - 2), 4 * g + 4))

            pend_norm = [None]

            def emit_norm():
                if pend_norm[0] is None:
                    return
                hh, bank, pe_cnt = pend_norm[0]
                pend_norm[0] = None
                em.wait("vector", "pe", pe_cnt)
                em.v(lambda e, bank=bank: e.reciprocal(
                    recd[:, :], bank[64:65, :]))
                em.wait("tensor", "v", em.now("v"))
                em.pe(lambda e: e.matmul(P[4][0:64, :],
                                         ones_row_f[0:1, 0:64], recd[:, :],
                                         start=True, stop=True))
                em.wait("vector", "pe", em.now("pe"))
                em.v(lambda e: e.tensor_copy(recb[0:64, :], P[4][0:64, :]))
                if hh % 2 == 0:
                    dsty = yT1[0:64, 512 * (hh // 2):512 * (hh // 2) + 512]
                else:
                    dsty = ystg1[0:64, 0:512]
                    em.wait("vector", "ld", ystg1_dma[0])
                em.v(lambda e, dsty=dsty, bank=bank: e.tensor_tensor(
                    dsty, bank[0:64, :], recb[0:64, :], OP.mult))
                bank_free_v[hh % 2] = em.now("v")
                if hh % 2 == 1:
                    em.wait("sync", "v", em.now("v"))
                    ystg1_dma[0] = em.dma(
                        "sync", yT1[64:128, 512 * (hh // 2):
                                    512 * (hh // 2) + 512],
                        ystg1[0:64, :], "ld")

            for h in range(16):
                kv = (0 if h < 8 else 2) + (h % 2)
                ktile, pofk = kv // 2, 64 * (kv % 2)
                pofq = 64 * (h % 2)
                ot_bank = P[2 + h % 2]
                def jgeo(ji):
                    jb = jbs[ji]
                    jloc = jb - 4 * g
                    if prefix == "g":
                        istart = 128 * jloc if jloc > 0 else 0
                        iend = 512
                    else:
                        istart = 128 * max(0, jloc)
                        iend = 128 * (min(3, jloc + 2) + 1)
                    return jb, jloc, istart, iend - istart

                def emit_score(ji):
                    jb, jloc, istart, W = jgeo(ji)
                    pi = ji % 2
                    em.wait("tensor", "sc", max(last_exp_sc[pi], sc_base[0]))
                    em.wait("tensor", "v", qprep_v)
                    em.pe(lambda e, pi=pi, W=W, jb=jb, ktile=ktile,
                          pofk=pofk, pofq=pofq, h=h, istart=istart: e.matmul(
                        P[pi][:, 0:W],
                        kT1[pofk:pofk + 64,
                            S * ktile + 128 * jb:S * ktile + 128 * (jb + 1)],
                        qT1[pofq:pofq + 64,
                            512 * (h // 2) + istart:
                            512 * (h // 2) + istart + W],
                        start=True, stop=True))

                if prefix == "l":
                    em.wait("tensor", "v", bank_free_v[h % 2])
                    em.pe(lambda e, ot_bank=ot_bank: e.matmul(
                        ot_bank[0:65, 0:512], ones_row_bf[0:1, 0:65],
                        zrow_bf[0:1, :], start=True, stop=False))
                emit_score(0)
                emit_norm()
                for ji, jb in enumerate(jbs):
                    jb, jloc, istart, W = jgeo(ji)
                    pi = ji % 2
                    em.wait("scalar", "pe", em.now("pe"))
                    em.sc(lambda e, pi=pi, W=W, pt=PT[pi % len(PT)]:
                          e.activation(
                        pt[:, 0:W], P[pi][:, 0:W], AF.Exp,
                        bias=neg8_b[:, :], scale=0.125))
                    scn = em.now("sc")
                    last_exp_sc[pi] = scn
                    need_v = False
                    if 0 <= jloc <= 3:
                        em.wait("vector", "sc", scn)
                        dcol = 128 * jloc - istart
                        em.v(lambda e, pi=pi, dcol=dcol,
                             pt=PT[pi % len(PT)]: e.tensor_tensor(
                            pt[:, dcol:dcol + 128],
                            pt[:, dcol:dcol + 128],
                            causal4[:, 0:128], OP.mult))
                        need_v = True
                    if prefix == "l" and jloc + 2 <= 3:
                        em.wait("vector", "sc", scn)
                        wcol = 128 * (jloc + 2) - istart
                        em.v(lambda e, pi=pi, wcol=wcol,
                             pt=PT[pi % len(PT)]: e.tensor_tensor(
                            pt[:, wcol:wcol + 128],
                            pt[:, wcol:wcol + 128],
                            winm[:, :], OP.mult))
                        need_v = True
                    vmask = em.now("v")
                    if ji + 1 < len(jbs):
                        emit_score(ji + 1)
                    if need_v:
                        em.wait("tensor", "v", vmask)
                    else:
                        em.wait("tensor", "sc", scn)
                    if ji == 0:
                        em.wait("tensor", "v", bank_free_v[h % 2])
                    em.pe(lambda e, jb=jb, kv=kv, W=W, istart=istart,
                          pi=pi, ji=ji, last=(ji == len(jbs) - 1),
                          st=(ji == 0 and prefix != "l"),
                          pt=PT[pi % len(PT)],
                          ot_bank=ot_bank: e.matmul(
                        ot_bank[0:65, istart:istart + W],
                        v1_sb[:, 260 * jb + 65 * kv:260 * jb + 65 * kv + 65],
                        pt[:, 0:W],
                        start=st, stop=last))
                pend_norm[0] = (h, ot_bank, em.now("pe"))
            emit_norm()
            em.wait("tensor", "ld", em.now("ld"))
            em.wait("tensor", "v", em.now("v"))
            for cl in range(4):
                c = 4 * g + cl
                for oh in range(2):
                    pb = P[oh]
                    em.wait("tensor", "v", em.now("v"))
                    for kt in range(KT):
                        em.pe(lambda e, pb=pb, kt=kt, cl=cl, oh=oh: e.matmul(
                            pb[:, :],
                            yT1[:, 512 * kt + 128 * cl:
                                512 * kt + 128 * (cl + 1)],
                            wo_sb[:, DM * kt + 512 * oh:
                                  DM * kt + 512 * (oh + 1)],
                            start=(kt == 0), stop=(kt == KT - 1)))
                    pen = em.now("pe")
                    em.wait("vector", "pe", pen)
                    aslice = a_sb[:, c * DM + 512 * oh:
                                  c * DM + 512 * (oh + 1)]
                    em.v(lambda e, aslice=aslice, pb=pb: e.tensor_add(
                        aslice, aslice, pb[:, :]))

    em.wait("vector", "sd", xsp_sd)
    em.v(lambda e: e.memset(
        ap3(v1_sb, 64, [[16 * 260, 128], [260, 16], [65, 4]]), 1.0))
    em.wait("vector", "v", gw_fold_v)
    if "g" in L1_PARTS:
        l1_attn("g", "ropes_g", True, gwq_bf, gwo_bf)

    AU2 = Arena(nc, AEND - SBUF_BASE)
    lwq_bf = AU2.take("lwq_bf", [128, KT * DM], BF16)
    lwk_bf = AU2.take("lwk_bf", [128, KT * 256], BF16)
    lwv_bf = AU2.take("lwv_bf", [128, KT * 256], BF16)
    lwo_bf = AU2.take("lwo_bf", [128, KT * DM], BF16)
    PT2 = [AU2.take(f"PT2{i}", [128, 512], BF16) for i in range(2)]
    ystg2 = AU2.take("ystg2", [64, 512], BF16)
    glob_pe, glob_v, glob_sc = em.now("pe"), em.now("v"), em.now("sc")
    em.wait("gpsimd", "pe", glob_pe)
    em.wait("gpsimd", "v", glob_v)
    em.wait("gpsimd", "sc", glob_sc)
    for wname, wsb, wid, nk in (("lwq", lwq_bf, DM, 8), ("lwk", lwk_bf, 256, 8),
                                ("lwv", lwv_bf, 256, 8), ("lwo", lwo_bf, DM, 8)):
        src = ap3(d[wname], 0, [[wid, 128], [128 * wid, nk], [1, wid]])
        dstw = ap3(wsb, 0, [[nk * wid, 128], [wid, nk], [1, wid]])
        em.dma("gpsimd", dstw, src, "gw")
    loc_w = em.now("gw")
    em.wait("vector", "gw", loc_w)
    for kt in range(KT):
        em.v(lambda e, kt=kt: e.tensor_scalar(
            lwq_bf[:, DM * kt:DM * (kt + 1)],
            lwq_bf[:, DM * kt:DM * (kt + 1)],
            n2w_sb[:, kt:kt + 1], None, OP.mult))
        em.v(lambda e, kt=kt: e.tensor_scalar(
            lwk_bf[:, 256 * kt:256 * (kt + 1)],
            lwk_bf[:, 256 * kt:256 * (kt + 1)],
            n2w_sb[:, kt:kt + 1], None, OP.mult))
        em.v(lambda e, kt=kt: e.tensor_scalar(
            lwv_bf[:, 256 * kt:256 * (kt + 1)],
            lwv_bf[:, 256 * kt:256 * (kt + 1)],
            n2w_sb[:, kt:kt + 1], None, OP.mult))
    PT_save = PT[:]
    PT[0], PT[1] = PT2[0], PT2[1]
    ystg1_save = ystg1
    ystg1 = ystg2
    if "l" in L1_PARTS:
        l1_attn("l", "ropes_l", False, lwq_bf, lwo_bf, wk_sb=lwk_bf,
                wv_sb=lwv_bf)
    PT[0], PT[1] = PT_save[0], PT_save[1]
    ystg1 = ystg1_save

    ARf = Arena(nc, 0, limit=65664)
    w1q = ARf.take("w1q", [128, KT * H1], FP8)
    w2q = ARf.take("w2q", [128, NH1 * DM], FP8)
    AF_ = Arena(nc, AEND - SBUF_BASE)
    xq8tb = [AF_.take(f"xq8tb{i}", [128, KT * 512], FP8) for i in range(2)]
    h8_g = AF_.take("h8_g", [128, NH1 * 512], FP8)
    h16b = [AF_.take(f"h16b{i}", [128, 512], FP16) for i in range(2)]
    y16b = [AF_.take(f"y16b{i}", [128, 512], FP16) for i in range(2)]
    fin = [AF_.take(f"fin{i}", [128, DM], F32) for i in range(2)]
    b2row8 = AF_.take("b2row8", [1, DM], FP8)
    b2row_bf = AF_.take("b2row_bf", [1, DM], BF16)
    b2bc = AF_.take("b2bc", [128, DM], F32)
    dropb = [AF_.take(f"dropb{i}", [128, DM], BF16) for i in range(2)]

    if "ffn" in L1_PARTS:
        loc_pe, loc_v, loc_sc = em.now("pe"), em.now("v"), em.now("sc")
        em.wait("gpsimd", "pe", loc_pe)
        em.wait("gpsimd", "v", loc_v)
        em.wait("gpsimd", "sc", loc_sc)
        em.wait("sync", "pe", loc_pe)
        em.wait("sync", "v", loc_v)
        b2_ld = em.dma("sync", rowstg[:, :], d["b2_row"].ap(), "ld")
        em.wait("scalar", "ld", b2_ld)
        em.wait("scalar", "v", loc_v)
        em.sc(lambda e: e.activation(b2row8[:, :], rowstg[:, :], AF.Copy))
        em.wait("vector", "sc", em.now("sc"))
        em.v(lambda e: e.tensor_copy(b2row_bf[:, :], b2row8[:, :]))
        em.wait("tensor", "v", em.now("v"))
        for oh in range(2):
            em.pe(lambda e, oh=oh: e.matmul(
                P[oh][:, :], ones_row_bf[:, :],
                b2row_bf[0:1, 512 * oh:512 * (oh + 1)], start=True, stop=True))
        em.wait("vector", "pe", em.now("pe"))
        for oh in range(2):
            em.v(lambda e, oh=oh: e.tensor_copy(
                b2bc[:, 512 * oh:512 * (oh + 1)], P[oh][:, :]))
        src = ap3(d["w1_t"], 0, [[H1, 128], [128 * H1, KT], [1, H1]])
        em.dma("gpsimd", ap3(w1q, 0, [[KT * H1, 128], [H1, KT], [1, H1]]), src,
               "gw")
        src = ap3(d["w2_t"], 0, [[DM, 128], [128 * DM, NH1], [1, DM]])
        em.dma("gpsimd", ap3(w2q, 0, [[NH1 * DM, 128], [DM, NH1], [1, DM]]), src,
               "gw")
        f_w = em.now("gw")
        xq_cons = [0, 0]
        drop_cons = [0, 0]
        ffn_cons_sc = {0: em.now("sc"), 1: em.now("sc")}
        ffn_v_guard = em.now("v")
        for g in range(NG):
            em.wait("sync", "pe", xq_cons[g % 2])
            srcx = ap3(d["xq8_hbm"], 512 * g,
                       [[S, 128], [128 * S, KT], [1, 512]])
            xql = em.dma("sync", ap3(xq8tb[g % 2], 0,
                                     [[KT * 512, 128], [512, KT], [1, 512]]),
                         srcx, "ld")
            em.wait("tensor", "ld", xql)
            em.wait("tensor", "gw", f_w)
            for ht in range(NH1):
                pb = P[ht % 2]
                em.wait("tensor", "sc", ffn_cons_sc[ht % 2])
                em.wait("tensor", "v", ffn_v_guard)
                for kt in range(KT):
                    em.pe(lambda e, pb=pb, ht=ht, kt=kt, g=g: e.matmul(
                        pb[:, :],
                        w1q[:, H1 * kt + 128 * ht:H1 * kt + 128 * (ht + 1)],
                        xq8tb[g % 2][:, 512 * kt:512 * (kt + 1)],
                        start=(kt == 0), stop=(kt == KT - 1)))
                pen = em.now("pe")
                em.wait("scalar", "pe", pen)
                em.wait("scalar", "v", em.now("v"))
                em.sc(lambda e, pb=pb, ht=ht: e.activation(
                    h16b[ht % 2][:, :], pb[:, :], AF.Identity,
                    bias=b1q[:, ht:ht + 1]))
                ffn_cons_sc[ht % 2] = em.now("sc")
                em.wait("vector", "sc", em.now("sc"))
                em.v(lambda e, ht=ht: e.tensor_scalar_max(
                    h8_g[:, 512 * ht:512 * (ht + 1)], h16b[ht % 2][:, :], 0.0))
            xq_cons[g % 2] = em.now("pe")
            for cl in range(4):
                c = 4 * g + cl
                em.wait("gpsimd", "v", drop_cons[c % 2])
                drl = em.dma("gpsimd", dropb[c % 2][:, :],
                             d["drop_c"].ap()[128 * c:128 * (c + 1), :], "gw")
                for oh in range(2):
                    pb = P[2 + oh]
                    em.wait("tensor", "v", em.now("v"))
                    for ht in range(NH1):
                        em.pe(lambda e, pb=pb, ht=ht, cl=cl, oh=oh: e.matmul(
                            pb[:, :],
                            h8_g[:, 512 * ht + 128 * cl:
                                 512 * ht + 128 * (cl + 1)],
                            w2q[:, DM * ht + 512 * oh:DM * ht + 512 * (oh + 1)],
                            start=(ht == 0), stop=(ht == NH1 - 1)))
                    pen = em.now("pe")
                    em.wait("vector", "pe", pen)
                    em.wait("vector", "gw", drl)
                    em.v(lambda e, pb=pb, oh=oh: e.tensor_tensor(
                        y16b[oh][:, :], pb[:, :],
                        b2bc[:, 512 * oh:512 * (oh + 1)], OP.add))
                    em.v(lambda e, oh=oh, c=c: e.tensor_tensor(
                        sA[:, 512 * oh:512 * (oh + 1)], y16b[oh][:, :],
                        dropb[c % 2][:, 512 * oh:512 * (oh + 1)], OP.mult))
                    aslice = a_sb[:, c * DM + 512 * oh:c * DM + 512 * (oh + 1)]
                    em.v(lambda e, aslice=aslice, oh=oh: e.tensor_add(
                        aslice, aslice, sA[:, 512 * oh:512 * (oh + 1)]))
                drop_cons[c % 2] = em.now("v")
        ffn_v = em.now("v")

    em.wait("sync", "pe", em.now("pe"))
    em.wait("sync", "v", em.now("v"))
    em.wait("sync", "sc", em.now("sc"))
    fin_cons = [0, 0]
    for c in range(NCH):
        em.wait("sync", "v", fin_cons[c % 2])
        fl = em.dma("sync", fin[c % 2][:, :],
                    d["x_hbm"].ap()[128 * c:128 * (c + 1), :], "ld")
        em.wait("vector", "ld", fl)
        em.wait("vector", "sd", em.now("sd"))
        em.v(lambda e, c=c: e.tensor_add(
            sB[:, :], fin[c % 2][:, :], a_sb[:, c * DM:(c + 1) * DM]))
        fin_cons[c % 2] = em.now("v")
        em.wait("scalar", "v", em.now("v"))
        em.dma("scalar", d["out_x"].ap()[128 * c:128 * (c + 1), :],
               sB[:, :], "sd")
    em.wait("scalar", "sd", em.now("sd"))

    em.play()
    ctx.close()
    return nc, d



def _host_consts():
    bf = ml_dtypes.bfloat16
    ident = np.eye(128, dtype=bf)
    j = np.arange(128)[:, None]
    i = np.arange(128)[None, :]
    causal = (j <= i).astype(np.float32)
    causal4 = np.ascontiguousarray(np.tile(causal, (1, 4))).astype(bf)
    win = (j > i).astype(np.float32).astype(bf)
    ropes = {}
    for name, base in (("ropes_g", 1e6), ("ropes_l", 1e4)):
        inv = 1.0 / (base ** (np.arange(0, 64, 2, dtype=np.float64) / 64.0))
        pos = np.arange(S, dtype=np.float64)
        fr = pos[:, None] * inv[None, :]
        cos = np.cos(fr).astype(np.float32).reshape(16, 128, 32)
        sin = np.sin(fr).astype(np.float32).reshape(16, 128, 32)
        cos = np.ascontiguousarray(cos.transpose(1, 0, 2).reshape(128, 512))
        sin = np.ascontiguousarray(sin.transpose(1, 0, 2).reshape(128, 512))
        ropes[name] = np.ascontiguousarray(
            np.concatenate([cos, sin], axis=1))
    return ident, causal4, win, ropes


def _rotl(x, r):
    return ((x << np.uint32(r)) | (x >> np.uint32(32 - r))).astype(np.uint32)


def _threefry2x32(k0, k1, x0, x1):
    ks0 = np.uint32(k0); ks1 = np.uint32(k1)
    ks2 = np.uint32(ks0 ^ ks1 ^ np.uint32(0x1BD11BDA))
    x0 = (x0 + ks0).astype(np.uint32); x1 = (x1 + ks1).astype(np.uint32)
    R0 = (13, 15, 26, 6); R1 = (17, 29, 16, 24)

    def rounds(x0, x1, rots):
        for r in rots:
            x0 = (x0 + x1).astype(np.uint32)
            x1 = _rotl(x1, r)
            x1 = (x1 ^ x0).astype(np.uint32)
        return x0, x1
    x0, x1 = rounds(x0, x1, R0)
    x0 = (x0 + ks1).astype(np.uint32); x1 = (x1 + ks2 + np.uint32(1)).astype(np.uint32)
    x0, x1 = rounds(x0, x1, R1)
    x0 = (x0 + ks2).astype(np.uint32); x1 = (x1 + ks0 + np.uint32(2)).astype(np.uint32)
    x0, x1 = rounds(x0, x1, R0)
    x0 = (x0 + ks0).astype(np.uint32); x1 = (x1 + ks1 + np.uint32(3)).astype(np.uint32)
    x0, x1 = rounds(x0, x1, R1)
    x0 = (x0 + ks1).astype(np.uint32); x1 = (x1 + ks2 + np.uint32(4)).astype(np.uint32)
    x0, x1 = rounds(x0, x1, R0)
    x0 = (x0 + ks2).astype(np.uint32); x1 = (x1 + ks0 + np.uint32(5)).astype(np.uint32)
    return x0, x1


def _host_drop():
    n = 8 * S * DM
    i = np.arange(n, dtype=np.uint64)
    c1 = (i >> np.uint64(32)).astype(np.uint32)
    c2 = (i & np.uint64(0xFFFFFFFF)).astype(np.uint32)
    b0, b1 = _threefry2x32(np.uint32(0), np.uint32(42), c1, c2)
    import os
    if os.environ.get("BASS_SIM_FP64") == "1":
        bits = (b0.astype(np.uint64) << np.uint64(32)) | b1.astype(np.uint64)
        u = ((bits >> np.uint64(12)) | np.uint64(0x3FF0000000000000)).view(np.float64) - 1.0
        keep = u < 0.8
    else:
        bits = (b0 ^ b1).astype(np.uint32)
        u = ((bits >> np.uint32(9)) | np.uint32(0x3F800000)).view(np.float32) - np.float32(1.0)
        keep = u < np.float32(0.8)
    keep = keep.reshape(8, S, DM)
    return np.where(keep, np.float32(1.25),
                    np.float32(0.0)).astype(ml_dtypes.bfloat16)


def kernel(**inputs):
    f32 = np.float32

    def T(a):
        return np.ascontiguousarray(np.asarray(a, f32).T)

    def col(a, n):
        return np.ascontiguousarray(np.asarray(a, f32).reshape(n, 128).T)

    def row(a):
        return np.ascontiguousarray(np.asarray(a, f32).reshape(1, -1))

    ident, causal4, win, ropes = _host_consts()
    drop = _host_drop()
    x = np.asarray(inputs["L0_attn_x"], f32)

    head_of_slot = []
    ev = [0, 1, 2, 3, 8, 9, 10, 11]
    odd = [4, 5, 6, 7, 12, 13, 14, 15]
    for i_ in range(8):
        head_of_slot.append(ev[i_])
        head_of_slot.append(odd[i_])
    qperm = np.concatenate([np.arange(64) + 64 * hh for hh in head_of_slot])

    def permq(w):
        w = np.asarray(w, f32)
        return np.ascontiguousarray(w[:, qperm])

    def permo(w):
        w = np.asarray(w, f32)
        return np.ascontiguousarray(w[qperm, :])

    shared = {
        "wq_t": T(inputs["L0_attn_Wq"]), "wk_t": T(inputs["L0_attn_Wk"]),
        "wv_t": T(inputs["L0_attn_Wv"]), "wo_t": T(inputs["L0_attn_Wo"]),
        "bq": col(inputs["L0_attn_bq"], 8), "bk": col(inputs["L0_attn_bk"], 8),
        "bv_row": row(inputs["L0_attn_bv"]), "bo_row": row(inputs["L0_attn_bo"]),
        "clip_x": np.asarray(inputs["L0_ffn_clip_x"], f32).reshape(1, 1),
        "clip_down": np.asarray(inputs["L0_ffn_clip_down"], f32).reshape(1, 1),
        "wg": np.ascontiguousarray(np.asarray(inputs["L0_ffn_w_gate"], f32)),
        "wu": np.ascontiguousarray(np.asarray(inputs["L0_ffn_w_up"], f32)),
        "wd": np.ascontiguousarray(np.asarray(inputs["L0_ffn_w_down"], f32)),
        "n1w": col(inputs["L1_attn_norm1_w"], 8),
        "n2w": col(inputs["L1_attn_norm2_w"], 8),
        "gwq": permq(inputs["L1_attn_g_wq"]),
        "gkvd": np.ascontiguousarray(np.asarray(inputs["L1_attn_g_kv_down"], f32)),
        "gkup": np.ascontiguousarray(np.asarray(inputs["L1_attn_g_k_up"], f32)),
        "gvup": np.ascontiguousarray(np.asarray(inputs["L1_attn_g_v_up"], f32)),
        "gwo": permo(inputs["L1_attn_g_wo"]),
        "lwq": permq(inputs["L1_attn_l_wq"]),
        "lwk": np.ascontiguousarray(np.asarray(inputs["L1_attn_l_wk"], f32)),
        "lwv": np.ascontiguousarray(np.asarray(inputs["L1_attn_l_wv"], f32)),
        "lwo": permo(inputs["L1_attn_l_wo"]),
        "w1_t": T(inputs["L1_ffn_w1"]), "b1": col(inputs["L1_ffn_b1"], 32),
        "w2_t": T(inputs["L1_ffn_w2"]), "b2_row": row(inputs["L1_ffn_b2"]),
        "ident_c": ident, "causal4_c": causal4, "win_c": win,
        "ropes_g": ropes["ropes_g"], "ropes_l": ropes["ropes_l"],
    }
    in_maps = []
    for i in range(NCORES):
        m = dict(shared)
        m["x_in"] = np.ascontiguousarray(x[i])
        m["drop_c"] = np.ascontiguousarray(drop[i])
        in_maps.append(m)

    nc, _ = build_nc(layers=RUN_LAYERS)
    res = run_bass_kernel_spmd(nc, in_maps, core_ids=list(range(NCORES)))
    out = np.stack([r["out_x"] for r in res.results], axis=0)
    return out.astype(np.float32)
